# version 02
# baseline (speedup 1.0000x reference)
import numpy as np
import ml_dtypes
import bass_rust
import concourse.bass as bass
import concourse.mybir as mybir
from concourse.alu_op_type import AluOpType as AO
from concourse.bass_utils import run_bass_kernel_spmd

F32 = mybir.dt.float32
BF16 = mybir.dt.bfloat16
I32 = mybir.dt.int32
AF = mybir.ActivationFunctionType
BF = ml_dtypes.bfloat16

NCORES = 8
DM = 1024
NC_DM = DM // 128
EPS = 1e-5


def mkap(t_ap, offset, dims):
    return bass_rust.AP(tensor=t_ap.tensor, offset=int(t_ap.offset) + int(offset),
                        ap=[list(d) for d in dims])


def sb_mid(ap, pre, post):
    pstride = ap.ap[0][0]
    return bass_rust.AP(tensor=ap.tensor, offset=int(ap.offset),
                        ap=[list(d) for d in pre] + [[pstride, 128]] + [list(d) for d in post])


class Buf:
    __slots__ = ("name", "w", "rs")
    def __init__(self, name):
        self.name = name
        self.w = None
        self.rs = []


QDEPTH = 6
COMPUTE_ENGS = ["PE", "DVE", "ACT", "POOL"]
DMA_QUEUES = ["QS", "QG", "QA"]
ALL_SEMS = COMPUTE_ENGS + [f"{q}{i}" for q in DMA_QUEUES for i in range(QDEPTH)]


class Prog:
    SEQ_OF = {"PE": "PE", "DVE": "DVE", "ACT": "ACT", "POOL": "POOL",
              "QS": "QS", "QG": "POOL", "QA": "ACT"}
    for _q in DMA_QUEUES:
        for _i in range(QDEPTH):
            SEQ_OF[f"{_q}{_i}"] = SEQ_OF[_q]

    def __init__(self, nc, block, sems):
        self.nc = nc
        self.block = block
        self.sem = sems
        self.count = {k: 0 for k in sems}
        self.qidx = {q: 0 for q in DMA_QUEUES}
        seqs = set(self.SEQ_OF.values())
        self.waited = {s: {k: 0 for k in sems} for s in seqs}
        self._eng = {"PE": block.tensor, "DVE": block.vector, "ACT": block.scalar,
                     "POOL": block.gpsimd, "QS": block.sync, "QG": block.gpsimd,
                     "QA": block.scalar}

    def buf(self, name):
        return Buf(name)

    def step(self, who, emit, reads=(), writes=()):
        eng = who
        if who in DMA_QUEUES:
            i = self.qidx[who]
            self.qidx[who] = (i + 1) % QDEPTH
            myname, sz = f"{who}{i}", 16
        else:
            myname, sz = who, 1
        seq = self.SEQ_OF[who]
        need = {}
        def _merge(tok):
            if tok is not None:
                s, v = tok
                if v > need.get(s, 0):
                    need[s] = v
        for b in reads:
            _merge(b.w)
        for b in writes:
            _merge(b.w)
            for t in b.rs:
                _merge(t)
        if sz == 16:
            _merge((myname, self.count[myname]))
        waits = []
        wc = self.waited[seq]
        for s, v in need.items():
            if v > wc[s]:
                assert v <= self.count[s], f"wait on future {s}={v} (emitted {self.count[s]})"
                waits.append((s, v))
                wc[s] = v
        self.count[myname] += sz
        tok = (myname, self.count[myname])
        sem = self.sem
        def _f(e, waits=waits, myname=myname, sz=sz):
            for s, v in waits:
                e.wait_ge(sem[s], v)
            inst = emit(e)
            inst.then_inc(sem[myname], sz)
        self._eng[eng](_f)
        for b in writes:
            b.w = tok
            b.rs = []
        for b in reads:
            b.rs.append(tok)
        return tok

    def drain(self, who):
        seq = self.SEQ_OF[who]
        wc = self.waited[seq]
        sem = self.sem
        waits = [(s, self.count[s]) for s in self.count if self.count[s] > wc[s]]
        for s, v in waits:
            wc[s] = v
        def _f(e):
            for s, v in waits:
                e.wait_ge(sem[s], v)
        self._eng[who](_f)



def tile_lhs(wT, mchunk=128):
    K, M = wT.shape
    assert M % mchunk == 0
    return np.ascontiguousarray(wT.reshape(K, M // mchunk, mchunk).transpose(1, 0, 2))


def col_tiles(v, chunk=128):
    return np.ascontiguousarray(np.asarray(v).reshape(-1, chunk).T)



def alloc_shared(nc, ctxs, T, big=True, moe=True, l1=True):
    B = {}
    def sb(name, shape, dt):
        cm = nc.sbuf_tensor(name, shape, dt)
        h = cm.__enter__()
        ctxs.append(cm)
        B[name] = h[:]
        B[name + "_h"] = h
        B[name + "_buf"] = Buf(name)
        return h
    def ps(name, shape, dt=F32):
        cm = nc.psum_tensor(name, shape, dt)
        h = cm.__enter__()
        ctxs.append(cm)
        return h
    sb("xres", [128, NC_DM * T], F32)
    sb("xln", [128, NC_DM * T], BF16)
    sb("sq", [128, 1024], F32)
    B["sq_bufs"] = [Buf("sq0"), Buf("sq1")]
    sb("srow", [1, 1664], F32)
    sb("ones_col", [128, 2], F32)
    sb("ones_colb", [128, 2], BF16)
    sb("ones_row", [1, 128], F32)
    sb("ones_rowb", [1, 128], BF16)
    sb("t512f", [128, 512], F32)
    for i in range(4):
        sb(f"t512a{i}", [128, 512], BF16)
        sb(f"t512b{i}", [128, 512], BF16)
    for i in range(4):
        sb(f"ws{i}", [128, 1024], BF16)
    if big:
        for i in range(2):
            sb(f"wbig{i}", [128, 4096], BF16)
        sb("abuf", [128, 16 * (T // 2)], BF16)
        a0 = nc.lookup_mloc("t512a0").addr
        w1a = nc.lookup_mloc("wbig1").addr
        assert w1a + 8192 - a0 == 32768, (a0, w1a)
        oh = nc.alloc_sbuf_tensor_at("ovl32", [128, 16384], BF16, offset=a0)
        B["ovl32"] = oh[:]
        B["ovl32_buf"] = Buf("ovl32")
    sb("cconst", [128, 4], F32)
    if moe:
        sb("gwork", [128, 48], F32)
        sb("glog", [128, 8], F32)
        sb("gatesT", [8, T], BF16)
        sb("rowcell", [1, T], BF16)
        sb("gse", [128, 16], F32)
        sb("gwork2", [128, 16], F32)
    sb("lnrows", [1, 2 * T], F32)
    if l1:
        sb("bca", [128, 512], BF16)
        sb("bcb", [128, 512], BF16)
        sb("bcc", [128, 512], BF16)
        sb("tf0", [128, 512], F32)
        a0 = nc.lookup_mloc("t512a0").addr
        w3 = nc.lookup_mloc("ws3").addr
        assert w3 + 2048 - a0 == 16384, (a0, w3)
        hh = nc.alloc_sbuf_tensor_at("hproc", [128, 16 * 512], BF16, offset=a0)
        B["hproc"] = hh[:]
        B["hproc_h"] = hh
        B["hproc_buf"] = Buf("hproc")
    sb("idn128", [128, 128], F32)
    B["psum"] = [ps(f"pb{i}", [128, 512]) for i in range(8)]
    B["psum_buf"] = [Buf(f"pb{i}") for i in range(8)]
    return B


def declare_common(nc, B):
    B["idn128d"] = nc.dram_tensor("idn128d", [128, 128], F32, kind="ExternalInput").ap()
    return ["idn128d"]


def emit_const_init(P, B):
    P.step("DVE", lambda e: e.memset(B["ones_col"][:, 0:1], 1.0),
           writes=[B["ones_col_buf"]])
    P.step("DVE", lambda e: e.memset(B["ones_row"][0:1, :], 1.0),
           writes=[B["ones_row_buf"]])
    P.step("DVE", lambda e: e.memset(B["cconst"][:, 0:1], 1.0 / 127.0),
           writes=[B["cconst_buf"]])
    P.step("DVE", lambda e: e.memset(B["ones_colb"][:, 0:1], 1.0),
           writes=[B["ones_colb_buf"]])
    P.step("DVE", lambda e: e.memset(B["ones_rowb"][0:1, :], 1.0),
           writes=[B["ones_rowb_buf"]])
    P.step("DVE", lambda e: e.memset(B["cconst"][:, 1:2], 0.125),
           writes=[B["cconst_buf"]])
    P.step("QS", lambda e: e.dma_start(out=B["idn128"], in_=B["idn128d"]),
           reads=[], writes=[B["idn128_buf"]])


def emit_load_fm(P, B, T, dram_ap, dst, dst_buf, dt_bytes=4, queue="QS"):
    P.step(queue, lambda e: e.dma_start(
        out=dst.rearrange("p (c t) -> p c t", c=NC_DM),
        in_=mkap(dram_ap, 0, [[T, 128], [128 * T, NC_DM], [1, T]])),
        reads=[], writes=[dst_buf])


def emit_store_fm(P, B, T, src, src_buf, dram_ap, queue="QS"):
    P.step(queue, lambda e: e.dma_start(
        out=mkap(dram_ap, 0, [[T, 128], [128 * T, NC_DM], [1, T]]),
        in_=src.rearrange("p (c t) -> p c t", c=NC_DM)),
        reads=[src_buf], writes=[])



def emit_ln(P, B, T, src, src_buf, dst, dst_buf):
    ones_col, ones_col_b = B["ones_col"], B["ones_col_buf"]
    ones_row, ones_row_b = B["ones_row"], B["ones_row_buf"]
    sq, sq_bufs = B["sq"], B["sq_bufs"]
    srow, srow_b = B["srow"], B["srow_buf"]
    pb, pbuf = B["psum"], B["psum_buf"]
    tmp, tmp_b = B["t512f"], B["t512f_buf"]
    inv = 1.0 / DM
    for tb in range(T // 512):
        pS, pSb = pb[0][0:1, :], pbuf[0]
        pQ, pQb = pb[1][0:1, :], pbuf[1]
        for c in range(NC_DM):
            xs = src[:, c * T + tb * 512: c * T + (tb + 1) * 512]
            sqs = sq[:, (c % 2) * 512:(c % 2) * 512 + 512]
            sqb = sq_bufs[c % 2]
            P.step("POOL", lambda e, xs=xs, sqs=sqs: e.tensor_tensor(
                out=sqs, in0=xs, in1=xs, op=AO.mult),
                reads=[src_buf], writes=[sqb])
            P.step("PE", lambda e, xs=xs, c=c: e.matmul(
                pS, ones_col[:, 0:1], xs, start=(c == 0), stop=(c == NC_DM - 1)),
                reads=[src_buf, ones_col_b], writes=[pSb])
            P.step("PE", lambda e, sqs=sqs, c=c: e.matmul(
                pQ, ones_col[:, 0:1], sqs, start=(c == 0), stop=(c == NC_DM - 1)),
                reads=[sqb, ones_col_b], writes=[pQb])
        P.step("DVE", lambda e: e.tensor_scalar(
            out=srow[0:1, 0:512], in0=pS, scalar1=inv, scalar2=None, op0=AO.mult),
            reads=[pSb], writes=[srow_b])
        P.step("DVE", lambda e: e.tensor_scalar(
            out=srow[0:1, 512:1024], in0=pQ, scalar1=inv, scalar2=EPS, op0=AO.mult, op1=AO.add),
            reads=[pQb], writes=[srow_b])
        P.step("DVE", lambda e: e.tensor_tensor(
            out=srow[0:1, 1024:1536], in0=srow[0:1, 0:512], in1=srow[0:1, 0:512], op=AO.mult),
            reads=[srow_b], writes=[srow_b])
        P.step("DVE", lambda e: e.tensor_tensor(
            out=srow[0:1, 512:1024], in0=srow[0:1, 512:1024], in1=srow[0:1, 1024:1536], op=AO.subtract),
            reads=[srow_b], writes=[srow_b])
        P.step("ACT", lambda e: e.activation(
            out=srow[0:1, 1024:1536], in_=srow[0:1, 512:1024], func=AF.Sqrt),
            reads=[srow_b], writes=[srow_b])
        lnr, lnr_b = B["lnrows"], B["lnrows_buf"]
        P.step("DVE", lambda e, tb=tb: e.reciprocal(
            out=lnr[0:1, tb * 512:(tb + 1) * 512], in_=srow[0:1, 1024:1536]),
            reads=[srow_b], writes=[lnr_b])
        P.step("DVE", lambda e, tb=tb: e.tensor_tensor(
            out=lnr[0:1, T + tb * 512:T + (tb + 1) * 512], in0=srow[0:1, 0:512],
            in1=lnr[0:1, tb * 512:(tb + 1) * 512], op=AO.mult),
            reads=[srow_b, lnr_b], writes=[lnr_b])
        pA, pAb = pb[2], pbuf[2]
        pB_, pBb = pb[3], pbuf[3]
        P.step("PE", lambda e, tb=tb: e.matmul(
            pA[:, :], ones_row[0:1, :], lnr[0:1, tb * 512:(tb + 1) * 512], start=True, stop=True),
            reads=[lnr_b, ones_row_b], writes=[pAb])
        P.step("PE", lambda e, tb=tb: e.matmul(
            pB_[:, :], ones_row[0:1, :], lnr[0:1, T + tb * 512:T + (tb + 1) * 512], start=True, stop=True),
            reads=[lnr_b, ones_row_b], writes=[pBb])
        for c in range(NC_DM):
            xs = src[:, c * T + tb * 512: c * T + (tb + 1) * 512]
            ds = dst[:, c * T + tb * 512: c * T + (tb + 1) * 512]
            P.step("DVE", lambda e, xs=xs: e.tensor_tensor(
                out=tmp[:, :], in0=xs, in1=pA[:, :], op=AO.mult),
                reads=[src_buf, pAb], writes=[tmp_b])
            P.step("DVE", lambda e, ds=ds: e.tensor_tensor(
                out=ds, in0=tmp[:, :], in1=pB_[:, :], op=AO.subtract),
                reads=[tmp_b, pBb], writes=[dst_buf])



def emit_l0ffn_scale_setup(P, B):
    sc, sc_b = B["l0f_sc"], B["l0f_sc_buf"]
    P.step("DVE", lambda e: e.tensor_scalar(
        out=sc[:, 128:136], in0=sc[:, 128:136], scalar1=0.5, scalar2=None, op0=AO.mult),
        reads=[sc_b], writes=[sc_b])


def emit_l0ffn(P, B, T):
    xln, xln_b = B["xln"], B["xln_buf"]
    pb, pbuf = B["psum"], B["psum_buf"]
    at = [B[f"t512a{i}"] for i in range(4)]
    atb = [B[f"t512a{i}_buf"] for i in range(4)]
    t1 = [B[f"t512b{i}"] for i in range(4)]
    t1b = [B[f"t512b{i}_buf"] for i in range(4)]
    scr0, scr0_b = B["scr0"], B["scr0_buf"]
    scr1, scr1_b = B["scr1"], B["scr1_buf"]
    sc, sc_b = B["l0f_sc"], B["l0f_sc_buf"]
    ws = [B[f"ws{i}"] for i in range(4)]
    wsb = [B[f"ws{i}_buf"] for i in range(4)]
    wbig = [B["wbig0"], B["wbig1"]]
    wbigb = [B["wbig0_buf"], B["wbig1_buf"]]
    abuf, abuf_b = B["abuf"], B["abuf_buf"]
    nblk = T // 512
    NH = 32
    slot_ctr = [0]

    def slot():
        s = slot_ctr[0] % 4
        slot_ctr[0] += 1
        return s

    def s1_load(ha, op):
        sl = op * 2 + (ha % 2)
        wd = B["l0f_w1"] if op == 0 else B["l0f_w2"]
        src_ap = mkap(wd, ha * DM * 128, [[128, 128], [128 * 128, NC_DM], [1, 128]])
        P.step("QG", lambda e, sl=sl, src_ap=src_ap: e.dma_start(
            out=ws[sl].rearrange("p (c m) -> p c m", c=NC_DM), in_=src_ap),
            reads=[], writes=[wsb[sl]])

    s1_load(0, 0); s1_load(0, 1)
    for ha in range(NH):
        sl1, sl2 = (ha % 2), 2 + (ha % 2)
        if ha + 1 < NH:
            s1_load(ha + 1, 0); s1_load(ha + 1, 1)
        for tb in range(nblk):
            r = tb % 2
            p1, p1b = pb[4 + r], pbuf[4 + r]
            p2, p2b = pb[6 + r], pbuf[6 + r]
            def _mm(e, sl, tb, pp):
                last = None
                for c in range(NC_DM):
                    last = e.matmul(pp[:, :], ws[sl][:, c * 128:(c + 1) * 128],
                                    xln[:, c * T + tb * 512:c * T + (tb + 1) * 512],
                                    start=(c == 0), stop=(c == NC_DM - 1))
                return last
            P.step("PE", lambda e, sl=sl1, tb=tb, pp=p1: _mm(e, sl, tb, pp),
                   reads=[wsb[sl1], xln_b], writes=[p1b])
            P.step("PE", lambda e, sl=sl2, tb=tb, pp=p2: _mm(e, sl, tb, pp),
                   reads=[wsb[sl2], xln_b], writes=[p2b])
            s = slot()
            P.step("ACT", lambda e, ha=ha, s=s, p1=p1: e.activation(
                out=t1[s][:, :], in_=p1[:, :], func=AF.Silu, scale=sc[:, ha:ha + 1]),
                reads=[p1b, sc_b], writes=[t1b[s]])
            P.step("DVE", lambda e, ha=ha, s=s, p2=p2: e.scalar_tensor_tensor(
                out=at[s][:, :], in0=p2[:, :], scalar=sc[:, 32 + ha:32 + ha + 1],
                in1=t1[s][:, :], op0=AO.mult, op1=AO.mult),
                reads=[p2b, t1b[s], sc_b], writes=[atb[s]])
            P.step("QS", lambda e, ha=ha, tb=tb, s=s: e.dma_start(
                out=mkap(scr0, ha * 128 * T + tb * 512, [[T, 128], [1, 512]]),
                in_=at[s][:, :]),
                reads=[atb[s]], writes=[scr0_b])

    TH = T // 2
    nq = TH // 512

    def big_load(wkey, mc, nk):
        sl = mc % 2
        src_ap = mkap(B[wkey], mc * nk * 128 * 128, [[128, 128], [128 * 128, nk], [1, 128]])
        P.step("QG", lambda e, sl=sl, src_ap=src_ap, nk=nk: e.dma_start(
            out=wbig[sl].rearrange("p (c m) -> p c m", c=nk), in_=src_ap),
            reads=[], writes=[wbigb[sl]])

    xlns = xln
    def habuf(c, q):
        if c < 16:
            return abuf[:, c * TH + q * 512: c * TH + (q + 1) * 512]
        return xlns[:, (c - 16) * TH + q * 512: (c - 16) * TH + (q + 1) * 512]

    def load_half(src_scr, src_scr_b, th):
        P.step("QS", lambda e, th=th: e.dma_start(
            out=abuf.rearrange("p (c t) -> p c t", c=16),
            in_=mkap(src_scr, th * TH, [[T, 128], [128 * T, 16], [1, TH]])),
            reads=[src_scr_b], writes=[abuf_b])
        P.step("QS", lambda e, th=th: e.dma_start(
            out=xlns.rearrange("p (c t) -> p c t", c=NC_DM * T // TH)[:, 0:16, :],
            in_=mkap(src_scr, 16 * 128 * T + th * TH, [[T, 128], [128 * T, 16], [1, TH]])),
            reads=[src_scr_b], writes=[xln_b])

    def hstage(wkey, s_off, src_scr, src_scr_b, dst_scr, dst_scr_b):
        for th in range(2):
            load_half(src_scr, src_scr_b, th)
            big_load(wkey, 0, NH)
            for hb in range(NH):
                sl = hb % 2
                if hb + 1 < NH:
                    big_load(wkey, hb + 1, NH)
                for q in range(nq):
                    r = q % 2
                    pp, ppb = pb[4 + r], pbuf[4 + r]
                    def _mm(e, sl=sl, q=q, pp=pp):
                        last = None
                        for c in range(NH):
                            last = e.matmul(pp[:, :], wbig[sl][:, c * 128:(c + 1) * 128],
                                            habuf(c, q),
                                            start=(c == 0), stop=(c == NH - 1))
                        return last
                    P.step("PE", _mm, reads=[wbigb[sl], abuf_b, xln_b], writes=[ppb])
                    s = slot()
                    P.step("ACT", lambda e, hb=hb, s=s, pp=pp: e.activation(
                        out=at[s][:, :], in_=pp[:, :], func=AF.Gelu,
                        scale=sc[:, s_off + hb:s_off + hb + 1]),
                        reads=[ppb, sc_b], writes=[atb[s]])
                    P.step("QS", lambda e, hb=hb, th=th, q=q, s=s: e.dma_start(
                        out=mkap(dst_scr, hb * 128 * T + th * TH + q * 512, [[T, 128], [1, 512]]),
                        in_=at[s][:, :]),
                        reads=[atb[s]], writes=[dst_scr_b])

    hstage("l0f_h1", 64, scr0, scr0_b, scr1, scr1_b)
    hstage("l0f_h2", 96, scr1, scr1_b, scr0, scr0_b)

    xres, xres_b = B["xres"], B["xres_buf"]
    for th in range(2):
        load_half(scr0, scr0_b, th)
        big_load("l0f_w3", 0, NH)
        for dc in range(NC_DM):
            sl = dc % 2
            if dc + 1 < NC_DM:
                big_load("l0f_w3", dc + 1, NH)
            for q in range(nq):
                r = q % 2
                pp, ppb = pb[4 + r], pbuf[4 + r]
                def _mm(e, sl=sl, q=q, pp=pp):
                    last = None
                    for c in range(NH):
                        last = e.matmul(pp[:, :], wbig[sl][:, c * 128:(c + 1) * 128],
                                        habuf(c, q),
                                        start=(c == 0), stop=(c == NH - 1))
                    return last
                P.step("PE", _mm, reads=[wbigb[sl], abuf_b, xln_b], writes=[ppb])
                xs = xres[:, dc * T + th * TH + q * 512: dc * T + th * TH + (q + 1) * 512]
                P.step("DVE", lambda e, dc=dc, xs=xs, pp=pp: e.scalar_tensor_tensor(
                    out=xs, in0=pp[:, :], scalar=sc[:, 128 + dc:128 + dc + 1],
                    in1=xs, op0=AO.mult, op1=AO.add),
                    reads=[ppb, sc_b, xres_b], writes=[xres_b])



def prep_l0ffn(inp):
    d = {}
    d["l0f_w1"] = tile_lhs(np.ascontiguousarray(np.asarray(inp["L0_ffn_w1_w"]).T))
    d["l0f_w2"] = tile_lhs(np.ascontiguousarray(np.asarray(inp["L0_ffn_w2_w"]).T))
    d["l0f_h1"] = tile_lhs(np.ascontiguousarray(np.asarray(inp["L0_ffn_h1_w"]).T))
    d["l0f_h2"] = tile_lhs(np.ascontiguousarray(np.asarray(inp["L0_ffn_h2_w"]).T))
    d["l0f_w3"] = tile_lhs(np.ascontiguousarray(np.asarray(inp["L0_ffn_w3_w"]).T))
    sc = np.zeros((128, 137), np.float32)
    sc[:, 0:32] = col_tiles(inp["L0_ffn_w1_s"])
    sc[:, 32:64] = col_tiles(inp["L0_ffn_w2_s"])
    sc[:, 64:96] = col_tiles(inp["L0_ffn_h1_s"])
    sc[:, 96:128] = col_tiles(inp["L0_ffn_h2_s"])
    sc[:, 128:136] = col_tiles(inp["L0_ffn_w3_s"])
    d["l0f_sc"] = sc
    return d


def declare_l0ffn(nc, B, T):
    names = []
    for nm, shape in [("l0f_w1", [32, DM, 128]), ("l0f_w2", [32, DM, 128]),
                      ("l0f_h1", [32, 4096, 128]), ("l0f_h2", [32, 4096, 128]),
                      ("l0f_w3", [8, 4096, 128])]:
        B[nm] = nc.dram_tensor(nm, shape, F32, kind="ExternalInput").ap()
        names.append(nm)
    B["l0f_sc_d"] = nc.dram_tensor("l0f_sc_d", [128, 137], F32, kind="ExternalInput").ap()
    names.append("l0f_sc_d")
    B["scr0"] = nc.dram_tensor("scr0", [4096, T], BF16).ap()
    B["scr0_buf"] = Buf("scr0")
    B["scr1"] = nc.dram_tensor("scr1", [4096, T], BF16).ap()
    B["scr1_buf"] = Buf("scr1")
    return names


def emit_l0ffn_load_consts(P, B):
    P.step("QS", lambda e: e.dma_start(out=B["l0f_sc"], in_=B["l0f_sc_d"]),
           reads=[], writes=[B["l0f_sc_buf"]])
    emit_l0ffn_scale_setup(P, B)



def build_dev_sublayer(T, kinds):
    nc = bass.Bass("TRN2", target_bir_lowering=False)
    ctxs = []
    xin = nc.dram_tensor("xin", [DM, T], F32, kind="ExternalInput").ap()
    xout = nc.dram_tensor("xout", [DM, T], F32, kind="ExternalOutput").ap()
    moe = any(k in kinds for k in ("l1ffn", "l3ffn"))
    l1 = "l1ffn" in kinds
    B = alloc_shared(nc, ctxs, T, moe=moe, l1=l1)
    declare_common(nc, B)
    need_scr = True
    if need_scr:
        B["scr0"] = nc.dram_tensor("scr0", [4096, T], BF16).ap()
        B["scr0_buf"] = Buf("scr0")
        B["scr1"] = nc.dram_tensor("scr1", [4096, T], BF16).ap()
        B["scr1_buf"] = Buf("scr1")
    for k in kinds:
        if k == "l0ffn":
            cm = nc.sbuf_tensor("l0f_sc", [128, 137], F32)
            B["l0f_sc"] = cm.__enter__()[:]
            ctxs.append(cm)
            B["l0f_sc_buf"] = Buf("l0f_sc")
            for nm, shape in [("l0f_w1", [32, DM, 128]), ("l0f_w2", [32, DM, 128]),
                              ("l0f_h1", [32, 4096, 128]), ("l0f_h2", [32, 4096, 128]),
                              ("l0f_w3", [8, 4096, 128])]:
                B[nm] = nc.dram_tensor(nm, shape, F32, kind="ExternalInput").ap()
            B["l0f_sc_d"] = nc.dram_tensor("l0f_sc_d", [128, 137], F32, kind="ExternalInput").ap()
        elif k == "l2ffn":
            declare_l2ffn(nc, B, T)
        elif k == "l3ffn":
            declare_l3ffn(nc, B, T, ctxs)
        elif k == "l1ffn":
            declare_l1ffn(nc, B, T, ctxs)
        elif k == "l2attn":
            declare_l2attn(nc, B, T, ctxs)
            B["l2a_out"] = nc.dram_tensor("l2a_out_ext", [2048, DM], F32, kind="ExternalOutput").ap()
        elif k == "l3attn":
            declare_l3attn(nc, B, T, ctxs)
            B["l3a_y"] = nc.dram_tensor("l3a_y_ext", [DM, T], F32, kind="ExternalOutput").ap()
        elif k == "l1attn":
            declare_l1attn(nc, B, T, ctxs)
    sems = {}
    for name in ALL_SEMS:
        scm = nc.semaphore("sem_" + name)
        sems[name] = scm.__enter__()
        ctxs.append(scm)
    bcm = nc.Block()
    block = bcm.__enter__()
    ctxs.append(bcm)
    P = Prog(nc, block, sems)
    emit_const_init(P, B)
    if "l0ffn" in kinds:
        emit_l0ffn_load_consts(P, B)
    emit_load_fm(P, B, T, xin, B["xres"], B["xres_buf"])
    for k in kinds:
        emit_ln(P, B, T, B["xres"], B["xres_buf"], B["xln"], B["xln_buf"])
        if k == "l0ffn":
            emit_l0ffn(P, B, T)
        elif k == "l2ffn":
            emit_l2ffn(P, B, T)
        elif k == "l3ffn":
            emit_l3ffn(P, B, T)
        elif k == "l1ffn":
            emit_l1ffn(P, B, T)
        elif k == "l2attn":
            emit_l2attn(P, B, T)
        elif k == "l3attn":
            emit_l3attn(P, B, T)
        elif k == "l1attn":
            emit_l1attn(P, B, T)
    emit_store_fm(P, B, T, B["xres"], B["xres_buf"], xout)
    P.drain("QS")
    for c in ctxs[::-1]:
        c.__exit__(None, None, None)
    return nc


def build_dev_l0ffn(T=2048):
    return build_dev_sublayer(T, ["l0ffn"])


def prep_l2ffn(inp):
    d = {}
    d["l2f_wg"] = tile_lhs(np.ascontiguousarray(np.asarray(inp["L2_ffn_w_gate"]).T))
    d["l2f_wu"] = tile_lhs(np.ascontiguousarray(np.asarray(inp["L2_ffn_w_up"]).T))
    d["l2f_wd"] = tile_lhs(np.ascontiguousarray(np.asarray(inp["L2_ffn_w_down"]).T))
    return d


def declare_l2ffn(nc, B, T):
    names = []
    for nm, shape in [("l2f_wg", [24, DM, 128]), ("l2f_wu", [24, DM, 128]),
                      ("l2f_wd", [8, 3072, 128])]:
        B[nm] = nc.dram_tensor(nm, shape, I32, kind="ExternalInput").ap()
        names.append(nm)
    return names


def emit_l2ffn(P, B, T):
    xln, xln_b = B["xln"], B["xln_buf"]
    xres, xres_b = B["xres"], B["xres_buf"]
    pb, pbuf = B["psum"], B["psum_buf"]
    at = [B[f"t512a{i}"] for i in range(4)]
    atb = [B[f"t512a{i}_buf"] for i in range(4)]
    t1 = [B[f"t512b{i}"] for i in range(4)]
    t1b = [B[f"t512b{i}_buf"] for i in range(4)]
    scr0, scr0_b = B["scr0"], B["scr0_buf"]
    ws = [B[f"ws{i}"] for i in range(4)]
    wsb = [B[f"ws{i}_buf"] for i in range(4)]
    wbig = [B["wbig0"], B["wbig1"]]
    wbigb = [B["wbig0_buf"], B["wbig1_buf"]]
    abuf, abuf_b = B["abuf"], B["abuf_buf"]
    cc, cc_b = B["cconst"], B["cconst_buf"]
    nblk = T // 512
    NH = 24
    sctr = [0]
    def slot():
        s = sctr[0] % 4
        sctr[0] += 1
        return s

    def s1_load(hc, op):
        sl = op * 2 + (hc % 2)
        wd = B["l2f_wg"] if op == 0 else B["l2f_wu"]
        src_ap = mkap(wd, hc * DM * 128, [[128, 128], [128 * 128, NC_DM], [1, 128]])
        P.step("QG", lambda e, sl=sl, src_ap=src_ap: e.dma_start(
            out=ws[sl].rearrange("p (c m) -> p c m", c=NC_DM), in_=src_ap),
            reads=[], writes=[wsb[sl]])

    s1_load(0, 0); s1_load(0, 1)
    for hc in range(NH):
        sl1, sl2 = (hc % 2), 2 + (hc % 2)
        if hc + 1 < NH:
            s1_load(hc + 1, 0); s1_load(hc + 1, 1)
        for tb in range(nblk):
            r = tb % 2
            p1, p1b = pb[4 + r], pbuf[4 + r]
            p2, p2b = pb[6 + r], pbuf[6 + r]
            def _mm(e, sl, tb, pp):
                last = None
                for c in range(NC_DM):
                    last = e.matmul(pp[:, :], ws[sl][:, c * 128:(c + 1) * 128],
                                    xln[:, c * T + tb * 512:c * T + (tb + 1) * 512],
                                    start=(c == 0), stop=(c == NC_DM - 1))
                return last
            P.step("PE", lambda e, sl=sl1, tb=tb, pp=p1: _mm(e, sl, tb, pp),
                   reads=[wsb[sl1], xln_b], writes=[p1b])
            P.step("PE", lambda e, sl=sl2, tb=tb, pp=p2: _mm(e, sl, tb, pp),
                   reads=[wsb[sl2], xln_b], writes=[p2b])
            s = slot()
            P.step("ACT", lambda e, s=s, p1=p1: e.activation(
                out=t1[s][:, :], in_=p1[:, :], func=AF.Silu, scale=cc[:, 0:1]),
                reads=[p1b, cc_b], writes=[t1b[s]])
            P.step("DVE", lambda e, s=s, p2=p2: e.tensor_tensor(
                out=at[s][:, :], in0=p2[:, :], in1=t1[s][:, :], op=AO.mult),
                reads=[p2b, t1b[s]], writes=[atb[s]])
            P.step("QS", lambda e, hc=hc, tb=tb, s=s: e.dma_start(
                out=mkap(scr0, hc * 128 * T + tb * 512, [[T, 128], [1, 512]]),
                in_=at[s][:, :]),
                reads=[atb[s]], writes=[scr0_b])

    TH = T // 2
    nq = TH // 512
    fscale = 0.5 / (127.0 * 127.0)
    def habuf(c, q):
        if c < 16:
            return abuf[:, c * TH + q * 512: c * TH + (q + 1) * 512]
        return xln[:, (c - 16) * TH + q * 512: (c - 16) * TH + (q + 1) * 512]
    for th in range(2):
        P.step("QS", lambda e, th=th: e.dma_start(
            out=abuf.rearrange("p (c t) -> p c t", c=16),
            in_=mkap(scr0, th * TH, [[T, 128], [128 * T, 16], [1, TH]])),
            reads=[scr0_b], writes=[abuf_b])
        P.step("QS", lambda e, th=th: e.dma_start(
            out=xln.rearrange("p (c t) -> p c t", c=NC_DM * T // TH)[:, 0:NH - 16, :],
            in_=mkap(scr0, 16 * 128 * T + th * TH, [[T, 128], [128 * T, NH - 16], [1, TH]])),
            reads=[scr0_b], writes=[xln_b])
        def wd_load(dc):
            sl = dc % 2
            src_ap = mkap(B["l2f_wd"], dc * 3072 * 128, [[128, 128], [128 * 128, NH], [1, 128]])
            P.step("QG", lambda e, sl=sl, src_ap=src_ap: e.dma_start(
                out=wbig[sl].rearrange("p (c m) -> p c m", c=32)[:, 0:NH, :], in_=src_ap),
                reads=[], writes=[wbigb[sl]])
        wd_load(0)
        for dc in range(NC_DM):
            sl = dc % 2
            if dc + 1 < NC_DM:
                wd_load(dc + 1)
            for q in range(nq):
                r = q % 2
                pp, ppb = pb[4 + r], pbuf[4 + r]
                def _mm(e, sl=sl, q=q, pp=pp):
                    last = None
                    for c in range(NH):
                        last = e.matmul(pp[:, :], wbig[sl][:, c * 128:(c + 1) * 128],
                                        habuf(c, q),
                                        start=(c == 0), stop=(c == NH - 1))
                    return last
                P.step("PE", _mm, reads=[wbigb[sl], abuf_b, xln_b], writes=[ppb])
                xs = xres[:, dc * T + th * TH + q * 512: dc * T + th * TH + (q + 1) * 512]
                P.step("DVE", lambda e, xs=xs, pp=pp: e.scalar_tensor_tensor(
                    out=xs, in0=pp[:, :], scalar=fscale,
                    in1=xs, op0=AO.mult, op1=AO.add),
                    reads=[ppb, xres_b], writes=[xres_b])



def emit_top2_from_logits(P, B, T, tile, glog, glog_b):
    g2, g2b = B["gwork"], B["gwork_buf"]
    pb, pbuf = B["psum"], B["psum_buf"]
    gatesT, gatesT_b = B["gatesT"], B["gatesT_buf"]
    idn, idn_b = B["idn128"], B["idn128_buf"]
    P.step("DVE", lambda e: e.reduce_max(out=g2[:, 0:1], in_=glog[:, 0:8], axis=mybir.AxisListType.X),
           reads=[glog_b], writes=[g2b])
    P.step("DVE", lambda e: e.tensor_scalar(
        out=g2[:, 8:16], in0=glog[:, 0:8], scalar1=g2[:, 0:1], scalar2=None, op0=AO.is_ge),
        reads=[glog_b, g2b], writes=[g2b])
    P.step("DVE", lambda e: e.scalar_tensor_tensor(
        out=g2[:, 16:24], in0=g2[:, 8:16], scalar=-1e30, in1=glog[:, 0:8],
        op0=AO.mult, op1=AO.add),
        reads=[g2b, glog_b], writes=[g2b])
    P.step("DVE", lambda e: e.reduce_max(out=g2[:, 1:2], in_=g2[:, 16:24], axis=mybir.AxisListType.X),
           reads=[g2b], writes=[g2b])
    P.step("DVE", lambda e: e.tensor_scalar(
        out=g2[:, 2:3], in0=g2[:, 0:1], scalar1=-1.0, scalar2=None, op0=AO.mult),
        reads=[g2b], writes=[g2b])
    P.step("ACT", lambda e: e.activation(
        out=g2[:, 24:32], in_=glog[:, 0:8], func=AF.Exp, bias=g2[:, 2:3]),
        reads=[glog_b, g2b], writes=[g2b])
    P.step("DVE", lambda e: e.tensor_scalar(
        out=g2[:, 8:16], in0=glog[:, 0:8], scalar1=g2[:, 1:2], scalar2=None, op0=AO.is_ge),
        reads=[glog_b, g2b], writes=[g2b])
    P.step("DVE", lambda e: e.tensor_tensor(
        out=g2[:, 24:32], in0=g2[:, 24:32], in1=g2[:, 8:16], op=AO.mult),
        reads=[g2b], writes=[g2b])
    P.step("DVE", lambda e: e.reduce_sum(out=g2[:, 3:4], in_=g2[:, 24:32], axis=mybir.AxisListType.X),
           reads=[g2b], writes=[g2b])
    P.step("DVE", lambda e: e.reciprocal(out=g2[:, 4:5], in_=g2[:, 3:4]),
           reads=[g2b], writes=[g2b])
    P.step("DVE", lambda e: e.tensor_scalar(
        out=g2[:, 24:32], in0=g2[:, 24:32], scalar1=g2[:, 4:5], scalar2=None, op0=AO.mult),
        reads=[g2b], writes=[g2b])
    pp, ppb = pb[3], pbuf[3]
    P.step("PE", lambda e: e.matmul(
        pp[0:8, 0:128], g2[:, 24:32], idn, start=True, stop=True, is_transpose=True),
        reads=[g2b, idn_b], writes=[ppb])
    P.step("DVE", lambda e, tile=tile: e.tensor_copy(
        out=gatesT[0:8, tile * 128:(tile + 1) * 128], in_=pp[0:8, 0:128]),
        reads=[ppb], writes=[gatesT_b])


def emit_gate_row_bc(P, B, e_idx, tb, dst_rowcell):
    gatesT, gatesT_b = B["gatesT"], B["gatesT_buf"]
    cell, cell_b = dst_rowcell
    P.step("QG", lambda e: e.dma_start(
        out=cell, in_=gatesT[e_idx:e_idx + 1, tb * 512:(tb + 1) * 512]),
        reads=[gatesT_b], writes=[cell_b])



def prep_l3ffn(inp):
    d = {}
    w1 = np.asarray(inp["L3_ffn_w1"])
    W1 = np.ascontiguousarray(w1.transpose(1, 0, 2).reshape(DM, 4096))
    d["l3f_w1"] = tile_lhs(W1)
    w2 = np.asarray(inp["L3_ffn_w2"])
    d["l3f_w2"] = tile_lhs(np.ascontiguousarray(w2.reshape(4096, DM)))
    d["l3f_b1c"] = col_tiles(np.asarray(inp["L3_ffn_b1"]).reshape(-1))
    d["l3f_b2"] = np.ascontiguousarray(np.asarray(inp["L3_ffn_b2"]))
    G = np.concatenate([np.asarray(inp["L3_ffn_centroids"]).T,
                        np.asarray(inp["L3_ffn_w_route"]).T], axis=1)
    d["l3f_gmat"] = np.ascontiguousarray(G.reshape(8, 128, 16).transpose(1, 0, 2).reshape(128, 128))
    return d


def declare_l3ffn(nc, B, T, ctxs):
    names = []
    for nm, shape, dt in [("l3f_w1", [32, DM, 128], F32), ("l3f_w2", [8, 4096, 128], F32),
                          ("l3f_b1c", [128, 32], F32), ("l3f_b2", [8, DM], F32),
                          ("l3f_gmat", [128, 128], F32)]:
        B[nm] = nc.dram_tensor(nm, shape, dt, kind="ExternalInput").ap()
        names.append(nm)
    for nm, shape, sdt in [("l3f_b1c_s", [128, 32], F32), ("l3f_gmat_s", [128, 128], F32),
                           ("l3f_b2_s", [8, DM], BF16)]:
        cm = nc.sbuf_tensor(nm, shape, sdt)
        h = cm.__enter__()
        ctxs.append(cm)
        B[nm] = h[:]
        B[nm + "_buf"] = Buf(nm)
    return names


def emit_l3ffn(P, B, T):
    xln, xln_b = B["xln"], B["xln_buf"]
    xres, xres_b = B["xres"], B["xres_buf"]
    pb, pbuf = B["psum"], B["psum_buf"]
    at = [B[f"t512a{i}"] for i in range(4)]
    atb = [B[f"t512a{i}_buf"] for i in range(4)]
    t1 = [B[f"t512b{i}"] for i in range(4)]
    t1b = [B[f"t512b{i}_buf"] for i in range(4)]
    scr0, scr0_b = B["scr0"], B["scr0_buf"]
    ws = [B[f"ws{i}"] for i in range(4)]
    wsb = [B[f"ws{i}_buf"] for i in range(4)]
    wbig = [B["wbig0"], B["wbig1"]]
    wbigb = [B["wbig0_buf"], B["wbig1_buf"]]
    abuf, abuf_b = B["abuf"], B["abuf_buf"]
    sq, sq_bufs = B["sq"], B["sq_bufs"]
    srow, srow_b = B["srow"], B["srow_buf"]
    g2, g2b = B["gwork"], B["gwork_buf"]
    glog, glog_b = B["glog"], B["glog_buf"]
    rc0 = (B["rowcell"][0:1, 0:T], B["rowcell_buf"])
    ones_col, ones_col_b = B["ones_col"], B["ones_col_buf"]
    ones_row, ones_row_b = B["ones_row"], B["ones_row_buf"]
    b1c, b1c_s = B["l3f_b1c_s"], B["l3f_b1c_s_buf"]
    gmat, gmat_b = B["l3f_gmat_s"], B["l3f_gmat_s_buf"]
    b2s, b2s_b = B["l3f_b2_s"], B["l3f_b2_s_buf"]
    gcs = B["sq"]
    gcs_b, gcs_b2 = B["sq_bufs"][0], B["sq_bufs"][1]
    nblk = T // 512
    NH = 32

    P.step("QS", lambda e: e.dma_start(out=b1c, in_=B["l3f_b1c"]), reads=[], writes=[b1c_s])
    P.step("QS", lambda e: e.dma_start(out=gmat, in_=B["l3f_gmat"]), reads=[], writes=[gmat_b])
    P.step("QG", lambda e: e.dma_start(out=b2s[0:8, :], in_=B["l3f_b2"]), reads=[], writes=[b2s_b])

    tmpf, tmpf_b = B["t512f"], B["t512f_buf"]
    pn8, pn8b = pb[0], pbuf[0]
    for c in range(NC_DM):
        P.step("POOL", lambda e, c=c: e.tensor_tensor(
            out=tmpf[:, c * 16:(c + 1) * 16], in0=gmat[:, c * 16:(c + 1) * 16],
            in1=gmat[:, c * 16:(c + 1) * 16], op=AO.mult),
            reads=[gmat_b], writes=[tmpf_b])
    for c in range(NC_DM):
        P.step("PE", lambda e, c=c: e.matmul(
            pn8[0:1, 0:16], ones_col[:, 0:1], tmpf[:, c * 16:(c + 1) * 16],
            start=(c == 0), stop=(c == NC_DM - 1)),
            reads=[tmpf_b, ones_col_b], writes=[pn8b])
    P.step("ACT", lambda e: e.activation(out=srow[0:1, 1528:1536], in_=pn8[0:1, 0:8], func=AF.Sqrt),
           reads=[pn8b], writes=[srow_b])
    P.step("DVE", lambda e: e.tensor_scalar(
        out=srow[0:1, 1528:1536], in0=srow[0:1, 1528:1536], scalar1=1e-12, scalar2=None, op0=AO.max),
        reads=[srow_b], writes=[srow_b])
    P.step("DVE", lambda e: e.reciprocal(out=srow[0:1, 1536:1544], in_=srow[0:1, 1528:1536]),
           reads=[srow_b], writes=[srow_b])
    pcn, pcnb = pb[3], pbuf[3]
    P.step("PE", lambda e: e.matmul(
        pcn[:, 0:8], ones_row[0:1, :], srow[0:1, 1536:1544], start=True, stop=True),
        reads=[srow_b, ones_row_b], writes=[pcnb])
    P.step("DVE", lambda e: e.tensor_copy(out=g2[:, 32:40], in_=pcn[:, 0:8]),
           reads=[pcnb], writes=[g2b])
    bc_cn_col = 32

    xres_, xres_b_ = B["xres"], B["xres_buf"]
    lnr, lnr_b = B["lnrows"], B["lnrows_buf"]
    pse, pseb = pb[0], pbuf[0]
    for c in range(NC_DM):
        P.step("PE", lambda e, c=c: e.matmul(
            pse[0:1, 0:16], ones_col[:, 0:1], gmat[:, c * 16:(c + 1) * 16],
            start=(c == 0), stop=(c == NC_DM - 1)),
            reads=[gmat_b, ones_col_b], writes=[pseb])
    P.step("DVE", lambda e: e.tensor_copy(out=srow[0:1, 1544:1560], in_=pse[0:1, 0:16]),
           reads=[pseb], writes=[srow_b])
    pse2, pse2b = pb[1], pbuf[1]
    P.step("PE", lambda e: e.matmul(
        pse2[:, 0:16], ones_row[0:1, :], srow[0:1, 1544:1560], start=True, stop=True),
        reads=[srow_b, ones_row_b], writes=[pse2b])
    P.step("DVE", lambda e: e.tensor_copy(out=B["gse"][:, 0:16], in_=pse2[:, 0:16]),
           reads=[pse2b], writes=[B["gse_buf"]])
    for tile in range(T // 128):
        tb = tile // 4
        pzc, pzcb = pb[1], pbuf[1]
        P.step("PE", lambda e, tile=tile: e.matmul(
            pzc[0:128, 0:1], lnr[0:1, tile * 128:(tile + 1) * 128], ones_col[0:1, 0:1],
            start=True, stop=True),
            reads=[lnr_b, ones_col_b], writes=[pzcb])
        P.step("DVE", lambda e: e.tensor_copy(out=g2[:, 40:41], in_=pzc[0:128, 0:1]),
               reads=[pzcb], writes=[g2b])
        P.step("PE", lambda e, tile=tile: e.matmul(
            pzc[0:128, 1:2], lnr[0:1, T + tile * 128:T + (tile + 1) * 128], ones_col[0:1, 0:1],
            start=True, stop=True),
            reads=[lnr_b, ones_col_b], writes=[pzcb])
        P.step("DVE", lambda e: e.tensor_copy(out=g2[:, 41:42], in_=pzc[0:128, 1:2]),
               reads=[pzcb], writes=[g2b])
        P.step("DVE", lambda e: e.tensor_tensor(
            out=g2[:, 42:43], in0=g2[:, 40:41], in1=g2[:, 40:41], op=AO.mult),
            reads=[g2b], writes=[g2b])
        P.step("DVE", lambda e: e.tensor_scalar(
            out=g2[:, 42:43], in0=g2[:, 42:43], scalar1=-DM * EPS, scalar2=float(DM),
            op0=AO.mult, op1=AO.add),
            reads=[g2b], writes=[g2b])
        P.step("ACT", lambda e: e.activation(out=g2[:, 43:44], in_=g2[:, 42:43], func=AF.Sqrt),
               reads=[g2b], writes=[g2b])
        P.step("DVE", lambda e: e.tensor_scalar(
            out=g2[:, 43:44], in0=g2[:, 43:44], scalar1=1e-12, scalar2=None, op0=AO.max),
            reads=[g2b], writes=[g2b])
        P.step("DVE", lambda e: e.reciprocal(out=g2[:, 44:45], in_=g2[:, 43:44]),
               reads=[g2b], writes=[g2b])
        pl, plb = pb[4], pbuf[4]
        def _mml(e, tile=tile, pl=pl):
            last = None
            for c in range(NC_DM):
                last = e.matmul(pl[:, 0:16], xres_[:, c * T + tile * 128:c * T + (tile + 1) * 128],
                                gmat[:, c * 16:(c + 1) * 16], start=(c == 0), stop=(c == NC_DM - 1))
            return last
        P.step("PE", _mml, reads=[xres_b_, gmat_b], writes=[plb])
        P.step("DVE", lambda e: e.tensor_scalar(
            out=g2[:, 24:32], in0=pl[:, 0:8], scalar1=g2[:, 40:41], scalar2=None, op0=AO.mult),
            reads=[plb, g2b], writes=[g2b])
        P.step("DVE", lambda e: e.tensor_scalar(
            out=glog[:, 0:8], in0=pl[:, 8:16], scalar1=g2[:, 40:41], scalar2=None, op0=AO.mult),
            reads=[plb, g2b], writes=[glog_b])
        P.step("DVE", lambda e: e.tensor_scalar(
            out=B["gwork2"][:, 0:16], in0=B["gse"][:, 0:16], scalar1=g2[:, 41:42], scalar2=None, op0=AO.mult),
            reads=[B["gse_buf"], g2b], writes=[B["gwork2_buf"]])
        P.step("DVE", lambda e: e.tensor_tensor(
            out=g2[:, 24:32], in0=g2[:, 24:32], in1=B["gwork2"][:, 0:8], op=AO.subtract),
            reads=[g2b, B["gwork2_buf"]], writes=[g2b])
        P.step("DVE", lambda e: e.tensor_tensor(
            out=glog[:, 0:8], in0=glog[:, 0:8], in1=B["gwork2"][:, 8:16], op=AO.subtract),
            reads=[glog_b, B["gwork2_buf"]], writes=[glog_b])
        P.step("DVE", lambda e: e.tensor_scalar(
            out=g2[:, 24:32], in0=g2[:, 24:32], scalar1=g2[:, 44:45], scalar2=None, op0=AO.mult),
            reads=[g2b], writes=[g2b])
        P.step("DVE", lambda e: e.tensor_tensor(
            out=g2[:, 24:32], in0=g2[:, 24:32], in1=g2[:, 32:40], op=AO.mult),
            reads=[g2b], writes=[g2b])
        P.step("DVE", lambda e: e.tensor_tensor(
            out=glog[:, 0:8], in0=g2[:, 24:32], in1=glog[:, 0:8], op=AO.add),
            reads=[g2b, glog_b], writes=[glog_b])
        emit_top2_from_logits(P, B, T, tile, glog, glog_b)

    def w1_load(he):
        sl = he % 2
        src_ap = mkap(B["l3f_w1"], he * DM * 128, [[128, 128], [128 * 128, NC_DM], [1, 128]])
        P.step("QG", lambda e, sl=sl, src_ap=src_ap: e.dma_start(
            out=ws[sl].rearrange("p (c m) -> p c m", c=NC_DM), in_=src_ap),
            reads=[], writes=[wsb[sl]])
    w1_load(0)
    cur_e = [-1]
    for he in range(NH):
        sl = he % 2
        e_idx = he // 4
        if he + 1 < NH:
            w1_load(he + 1)
        if e_idx != cur_e[0]:
            cur_e[0] = e_idx
            P.step("QG", lambda e, e_idx=e_idx: e.dma_start(
                out=rc0[0], in_=B["gatesT"][e_idx:e_idx + 1, 0:T]),
                reads=[B["gatesT_buf"]], writes=[rc0[1]])
        for tb in range(nblk):
            pg, pgb = pb[2], pbuf[2]
            P.step("PE", lambda e, tb=tb: e.matmul(
                pg[:, :], B["ones_rowb"][0:1, :], rc0[0][0:1, tb * 512:(tb + 1) * 512], start=True, stop=True),
                reads=[rc0[1], B["ones_rowb_buf"]], writes=[pgb])
            r = tb % 2
            pp, ppb = pb[4 + r], pbuf[4 + r]
            def _mm(e, sl=sl, tb=tb, pp=pp):
                last = None
                for c in range(NC_DM):
                    last = e.matmul(pp[:, :], ws[sl][:, c * 128:(c + 1) * 128],
                                    xln[:, c * T + tb * 512:c * T + (tb + 1) * 512],
                                    start=(c == 0), stop=(c == NC_DM - 1))
                return last
            P.step("PE", _mm, reads=[wsb[sl], xln_b], writes=[ppb])
            s = he % 4
            P.step("ACT", lambda e, he=he, s=s, pp=pp: e.activation(
                out=t1[s][:, :], in_=pp[:, :], func=AF.Gelu, bias=b1c[:, he:he + 1]),
                reads=[ppb, b1c_s], writes=[t1b[s]])
            P.step("DVE", lambda e, s=s, pg=pg: e.tensor_tensor(
                out=at[s][:, :], in0=t1[s][:, :], in1=pg[:, :], op=AO.mult),
                reads=[t1b[s], pgb], writes=[atb[s]])
            P.step("QS", lambda e, he=he, tb=tb, s=s: e.dma_start(
                out=mkap(scr0, he * 128 * T + tb * 512, [[T, 128], [1, 512]]),
                in_=at[s][:, :]),
                reads=[atb[s]], writes=[scr0_b])

    TH = T // 2
    nq = TH // 512
    def habuf(c, q):
        if c < 16:
            return abuf[:, c * TH + q * 512: c * TH + (q + 1) * 512]
        return xln[:, (c - 16) * TH + q * 512: (c - 16) * TH + (q + 1) * 512]
    for th in range(2):
        P.step("QS", lambda e, th=th: e.dma_start(
            out=abuf.rearrange("p (c t) -> p c t", c=16),
            in_=mkap(scr0, th * TH, [[T, 128], [128 * T, 16], [1, TH]])),
            reads=[scr0_b], writes=[abuf_b])
        P.step("QS", lambda e, th=th: e.dma_start(
            out=xln.rearrange("p (c t) -> p c t", c=NC_DM * T // TH)[:, 0:16, :],
            in_=mkap(scr0, 16 * 128 * T + th * TH, [[T, 128], [128 * T, 16], [1, TH]])),
            reads=[scr0_b], writes=[xln_b])
        def w2_load(dc):
            sl = dc % 2
            src_ap = mkap(B["l3f_w2"], dc * 4096 * 128, [[128, 128], [128 * 128, NH], [1, 128]])
            P.step("QG", lambda e, sl=sl, src_ap=src_ap: e.dma_start(
                out=wbig[sl].rearrange("p (c m) -> p c m", c=NH), in_=src_ap),
                reads=[], writes=[wbigb[sl]])
        w2_load(0)
        for dc in range(NC_DM):
            sl = dc % 2
            if dc + 1 < NC_DM:
                w2_load(dc + 1)
            for q in range(nq):
                tb = (th * TH + q * 512) // 512
                r = q % 2
                pp, ppb = pb[4 + r], pbuf[4 + r]
                def _mm(e, sl=sl, q=q, pp=pp, dc=dc, tb=tb):
                    last = None
                    for c in range(NH):
                        last = e.matmul(pp[:, :], wbig[sl][:, c * 128:(c + 1) * 128],
                                        habuf(c, q),
                                        start=(c == 0), stop=False)
                    last = e.matmul(pp[:, :], b2s[0:8, dc * 128:(dc + 1) * 128],
                                    B["gatesT"][0:8, tb * 512:(tb + 1) * 512],
                                    start=False, stop=True)
                    return last
                P.step("PE", _mm, reads=[wbigb[sl], abuf_b, xln_b, b2s_b, B["gatesT_buf"]], writes=[ppb])
                xs = xres[:, dc * T + th * TH + q * 512: dc * T + th * TH + (q + 1) * 512]
                P.step("DVE", lambda e, xs=xs, pp=pp: e.scalar_tensor_tensor(
                    out=xs, in0=pp[:, :], scalar=0.5,
                    in1=xs, op0=AO.mult, op1=AO.add),
                    reads=[ppb, xres_b], writes=[xres_b])





def prep_l1ffn(inp):
    d = {}
    w1 = np.asarray(inp["L1_ffn_w1"])
    W1 = np.ascontiguousarray(w1.transpose(1, 0, 2).reshape(DM, 16384))
    d["l1f_w1"] = tile_lhs(W1)
    w2 = np.asarray(inp["L1_ffn_w2"])
    W2 = w2.reshape(16384, DM)
    d["l1f_w2"] = np.ascontiguousarray(W2.reshape(16384, 8, 128).transpose(1, 0, 2))
    d["l1f_b1c"] = col_tiles(np.asarray(inp["L1_ffn_b1"]).reshape(-1))
    d["l1f_g1c"] = col_tiles(np.asarray(inp["L1_ffn_ln1_g"]).reshape(-1))
    d["l1f_bb1c"] = col_tiles(np.asarray(inp["L1_ffn_ln1_b"]).reshape(-1))
    d["l1f_b2"] = np.ascontiguousarray(np.asarray(inp["L1_ffn_b2"]))
    gw = np.asarray(inp["L1_ffn_gate_w"])
    d["l1f_gw"] = np.ascontiguousarray(gw.reshape(8, 128, 8).transpose(1, 0, 2).reshape(128, 64))
    lngb = np.zeros((128, 16), np.float32)
    lngb[:, 0:8] = col_tiles(inp["L1_ffn_ln_g"])
    lngb[:, 8:16] = col_tiles(inp["L1_ffn_ln_b"])
    d["l1f_lngb"] = lngb
    return d


def declare_l1ffn(nc, B, T, ctxs):
    names = []
    for nm, shape, dt in [("l1f_w1", [128, DM, 128], F32), ("l1f_w2", [8, 16384, 128], F32),
                          ("l1f_b1c", [128, 128], F32), ("l1f_g1c", [128, 128], F32),
                          ("l1f_bb1c", [128, 128], F32), ("l1f_b2", [8, DM], F32),
                          ("l1f_gw", [128, 64], F32), ("l1f_lngb", [128, 16], F32)]:
        B[nm] = nc.dram_tensor(nm, shape, dt, kind="ExternalInput").ap()
        names.append(nm)
    for nm, shape, sdt in [("l1f_b1c_s", [128, 128], F32), ("l1f_g1c_s", [128, 128], F32),
                           ("l1f_bb1c_s", [128, 128], F32), ("l1f_b2_s", [8, DM], BF16),
                           ("l1f_gw_s", [128, 64], F32), ("l1f_lngb_s", [128, 16], F32)]:
        cm = nc.sbuf_tensor(nm, shape, sdt)
        h = cm.__enter__()
        ctxs.append(cm)
        B[nm] = h[:]
        B[nm + "_buf"] = Buf(nm)
    if "l1f_out" not in B:
        B["l1f_out"] = nc.dram_tensor("l1f_out", [DM, T], F32).ap()
        B["l1f_out_buf"] = Buf("l1f_out")
        B["l1f_v"] = nc.dram_tensor("l1f_v", [DM, T], BF16).ap()
        B["l1f_v_buf"] = Buf("l1f_v")
    return names


def emit_l1ffn(P, B, T):
    xln, xln_b = B["xln"], B["xln_buf"]
    xres, xres_b = B["xres"], B["xres_buf"]
    pb, pbuf = B["psum"], B["psum_buf"]
    at = [B[f"t512a{i}"] for i in range(4)]
    atb = [B[f"t512a{i}_buf"] for i in range(4)]
    t1 = [B[f"t512b{i}"] for i in range(4)]
    t1b = [B[f"t512b{i}_buf"] for i in range(4)]
    scr0, scr0_b = B["scr0"], B["scr0_buf"]
    ws = [B[f"ws{i}"] for i in range(4)]
    wsb = [B[f"ws{i}_buf"] for i in range(4)]
    wbig = [B["wbig0"], B["wbig1"]]
    wbigb = [B["wbig0_buf"], B["wbig1_buf"]]
    abuf, abuf_b = B["abuf"], B["abuf_buf"]
    sq, sq_bufs = B["sq"], B["sq_bufs"]
    srow, srow_b = B["srow"], B["srow_buf"]
    g2, g2b = B["gwork"], B["gwork_buf"]
    glog, glog_b = B["glog"], B["glog_buf"]
    lnr, lnr_b = B["lnrows"], B["lnrows_buf"]
    rc0 = (B["rowcell"][0:1, 0:T], B["rowcell_buf"])
    ones_col, ones_col_b = B["ones_col"], B["ones_col_buf"]
    ones_row, ones_row_b = B["ones_row"], B["ones_row_buf"]
    gw, gw_b = B["l1f_gw_s"], B["l1f_gw_s_buf"]
    b1c, b1c_b = B["l1f_b1c_s"], B["l1f_b1c_s_buf"]
    g1c, g1c_b = B["l1f_g1c_s"], B["l1f_g1c_s_buf"]
    bb1c, bb1c_b = B["l1f_bb1c_s"], B["l1f_bb1c_s_buf"]
    b2s, b2s_b = B["l1f_b2_s"], B["l1f_b2_s_buf"]
    lngb, lngb_b = B["l1f_lngb_s"], B["l1f_lngb_s_buf"]
    od, od_b = B["l1f_out"], B["l1f_out_buf"]
    vv, vv_b = B["l1f_v"], B["l1f_v_buf"]
    nblk = T // 512
    NHE = 16
    TH = T // 2
    nq = TH // 512

    for nm, key, kb in [("l1f_b1c", b1c, b1c_b), ("l1f_g1c", g1c, g1c_b),
                        ("l1f_bb1c", bb1c, bb1c_b),
                        ("l1f_gw", gw, gw_b), ("l1f_lngb", lngb, lngb_b)]:
        def _ld(e, key=key, nm=nm):
            return e.dma_start(out=key, in_=B[nm])
        P.step("QS", _ld, reads=[], writes=[kb])
    P.step("QG", lambda e: e.dma_start(out=b2s[0:8, :], in_=B["l1f_b2"]),
           reads=[], writes=[b2s_b])

    psg, psgb = pb[0], pbuf[0]
    for c in range(NC_DM):
        P.step("PE", lambda e, c=c: e.matmul(
            psg[0:1, 0:8], ones_col[:, 0:1], gw[:, c * 8:(c + 1) * 8],
            start=(c == 0), stop=(c == NC_DM - 1)),
            reads=[gw_b, ones_col_b], writes=[psgb])
    P.step("DVE", lambda e: e.tensor_copy(out=srow[0:1, 1544:1552], in_=psg[0:1, 0:8]),
           reads=[psgb], writes=[srow_b])
    psg2, psg2b = pb[1], pbuf[1]
    P.step("PE", lambda e: e.matmul(
        psg2[:, 0:8], ones_row[0:1, :], srow[0:1, 1544:1552], start=True, stop=True),
        reads=[srow_b, ones_row_b], writes=[psg2b])
    P.step("DVE", lambda e: e.tensor_copy(out=B["gse"][:, 0:8], in_=psg2[:, 0:8]),
           reads=[psg2b], writes=[B["gse_buf"]])
    for tile in range(T // 128):
        pzc, pzcb = pb[1], pbuf[1]
        P.step("PE", lambda e, tile=tile: e.matmul(
            pzc[0:128, 0:1], lnr[0:1, tile * 128:(tile + 1) * 128], ones_col[0:1, 0:1],
            start=True, stop=True),
            reads=[lnr_b, ones_col_b], writes=[pzcb])
        P.step("DVE", lambda e: e.tensor_copy(out=g2[:, 40:41], in_=pzc[0:128, 0:1]),
               reads=[pzcb], writes=[g2b])
        P.step("PE", lambda e, tile=tile: e.matmul(
            pzc[0:128, 1:2], lnr[0:1, T + tile * 128:T + (tile + 1) * 128], ones_col[0:1, 0:1],
            start=True, stop=True),
            reads=[lnr_b, ones_col_b], writes=[pzcb])
        P.step("DVE", lambda e: e.tensor_copy(out=g2[:, 41:42], in_=pzc[0:128, 1:2]),
               reads=[pzcb], writes=[g2b])
        pl, plb = pb[4], pbuf[4]
        def _mml(e, tile=tile, pl=pl):
            last = None
            for c in range(NC_DM):
                last = e.matmul(pl[:, 0:8], xres[:, c * T + tile * 128:c * T + (tile + 1) * 128],
                                gw[:, c * 8:(c + 1) * 8], start=(c == 0), stop=(c == NC_DM - 1))
            return last
        P.step("PE", _mml, reads=[xres_b, gw_b], writes=[plb])
        P.step("DVE", lambda e: e.tensor_scalar(
            out=glog[:, 0:8], in0=pl[:, 0:8], scalar1=g2[:, 40:41], scalar2=None, op0=AO.mult),
            reads=[plb, g2b], writes=[glog_b])
        P.step("DVE", lambda e: e.tensor_scalar(
            out=B["gwork2"][:, 0:8], in0=B["gse"][:, 0:8], scalar1=g2[:, 41:42], scalar2=None, op0=AO.mult),
            reads=[B["gse_buf"], g2b], writes=[B["gwork2_buf"]])
        P.step("DVE", lambda e: e.tensor_tensor(
            out=glog[:, 0:8], in0=glog[:, 0:8], in1=B["gwork2"][:, 0:8], op=AO.subtract),
            reads=[glog_b, B["gwork2_buf"]], writes=[glog_b])
        emit_top2_from_logits(P, B, T, tile, glog, glog_b)

    for ei in range(8):
        reg = (ei % 2) * 2048
        regb = scr0_b
        def w1_load(hc, ei=ei):
            sl = hc % 2
            he = ei * NHE + hc
            src_ap = mkap(B["l1f_w1"], he * DM * 128, [[128, 128], [128 * 128, NC_DM], [1, 128]])
            P.step("QG", lambda e, sl=sl, src_ap=src_ap: e.dma_start(
                out=ws[sl].rearrange("p (c m) -> p c m", c=NC_DM), in_=src_ap),
                reads=[], writes=[wsb[sl], B["hproc_buf"]])
        w1_load(0)
        for hc in range(NHE):
            sl = hc % 2
            he = ei * NHE + hc
            if hc + 1 < NHE:
                w1_load(hc + 1)
            for tb in range(nblk):
                r = tb % 2
                pp, ppb = pb[4 + r], pbuf[4 + r]
                def _mm(e, sl=sl, tb=tb, pp=pp):
                    last = None
                    for c in range(NC_DM):
                        last = e.matmul(pp[:, :], ws[sl][:, c * 128:(c + 1) * 128],
                                        xln[:, c * T + tb * 512:c * T + (tb + 1) * 512],
                                        start=(c == 0), stop=(c == NC_DM - 1))
                    return last
                P.step("PE", _mm, reads=[wsb[sl], xln_b], writes=[ppb])
                s = (hc * nblk + tb) % 4
                P.step("DVE", lambda e, he=he, s=s, pp=pp: e.tensor_scalar(
                    out=at[s][:, :], in0=pp[:, :], scalar1=b1c[:, he:he + 1], scalar2=None, op0=AO.add),
                    reads=[ppb, b1c_b], writes=[atb[s], B["hproc_buf"]])
                P.step("QS", lambda e, hc=hc, tb=tb, s=s, reg=reg: e.dma_start(
                    out=mkap(scr0, (reg + hc * 128) * T + tb * 512, [[T, 128], [1, 512]]),
                    in_=at[s][:, :]),
                    reads=[atb[s]], writes=[regb])

        P.step("QG", lambda e, ei=ei: e.dma_start(
            out=rc0[0], in_=B["gatesT"][ei:ei + 1, 0:T]),
            reads=[B["gatesT_buf"]], writes=[rc0[1]])

        for th in range(2):
            P.step("QS", lambda e, th=th, reg=reg: e.dma_start(
                out=abuf.rearrange("p (c t) -> p c t", c=16),
                in_=mkap(scr0, reg * T + th * TH, [[T, 128], [128 * T, NHE], [1, TH]])),
                reads=[regb], writes=[abuf_b])
            for hc in range(NHE):
                for q in range(nq):
                    hs = abuf[:, hc * TH + q * 512: hc * TH + (q + 1) * 512]
                    sqs = sq[:, (hc % 2) * 512:(hc % 2) * 512 + 512]
                    P.step("POOL", lambda e, hs=hs, sqs=sqs: e.tensor_tensor(
                        out=sqs, in0=hs, in1=hs, op=AO.mult),
                        reads=[abuf_b], writes=[sq_bufs[hc % 2]])
                    P.step("PE", lambda e, hs=hs, hc=hc, q=q: e.matmul(
                        pb[0 + q][0:1, :], B["ones_colb"][:, 0:1], hs,
                        start=(hc == 0), stop=(hc == NHE - 1)),
                        reads=[abuf_b, B["ones_colb_buf"]], writes=[pbuf[0 + q]])
                    P.step("PE", lambda e, sqs=sqs, hc=hc, q=q: e.matmul(
                        pb[2 + q][0:1, :], ones_col[:, 0:1], sqs,
                        start=(hc == 0), stop=(hc == NHE - 1)),
                        reads=[sq_bufs[hc % 2], ones_col_b], writes=[pbuf[2 + q]])
            for q in range(nq):
                P.step("DVE", lambda e, q=q: e.tensor_scalar(
                    out=srow[0:1, 0:512], in0=pb[0 + q][0:1, :], scalar1=1.0 / 2048, scalar2=None, op0=AO.mult),
                    reads=[pbuf[0 + q]], writes=[srow_b])
                P.step("DVE", lambda e, q=q: e.tensor_scalar(
                    out=srow[0:1, 512:1024], in0=pb[2 + q][0:1, :], scalar1=1.0 / 2048, scalar2=EPS,
                    op0=AO.mult, op1=AO.add),
                    reads=[pbuf[2 + q]], writes=[srow_b])
                P.step("DVE", lambda e: e.tensor_tensor(
                    out=srow[0:1, 1024:1536], in0=srow[0:1, 0:512], in1=srow[0:1, 0:512], op=AO.mult),
                    reads=[srow_b], writes=[srow_b])
                P.step("DVE", lambda e: e.tensor_tensor(
                    out=srow[0:1, 512:1024], in0=srow[0:1, 512:1024], in1=srow[0:1, 1024:1536], op=AO.subtract),
                    reads=[srow_b], writes=[srow_b])
                P.step("ACT", lambda e: e.activation(
                    out=srow[0:1, 1024:1536], in_=srow[0:1, 512:1024], func=AF.Sqrt),
                    reads=[srow_b], writes=[srow_b])
                P.step("DVE", lambda e, q=q: e.reciprocal(
                    out=lnr[0:1, q * 512:(q + 1) * 512], in_=srow[0:1, 1024:1536]),
                    reads=[srow_b], writes=[lnr_b])
                P.step("DVE", lambda e, q=q: e.tensor_tensor(
                    out=srow[0:1, 0:512], in0=srow[0:1, 0:512],
                    in1=lnr[0:1, q * 512:(q + 1) * 512], op=AO.mult),
                    reads=[srow_b, lnr_b], writes=[srow_b])
                P.step("DVE", lambda e, q=q: e.tensor_copy(
                    out=lnr[0:1, T + q * 512:T + (q + 1) * 512], in_=srow[0:1, 0:512]),
                    reads=[srow_b], writes=[lnr_b])
            for q in range(nq):
                P.step("PE", lambda e, q=q: e.matmul(
                    pb[4][:, :], ones_row[0:1, :], lnr[0:1, q * 512:(q + 1) * 512],
                    start=True, stop=True),
                    reads=[lnr_b, ones_row_b], writes=[pbuf[4]])
                P.step("PE", lambda e, q=q: e.matmul(
                    pb[5][:, :], ones_row[0:1, :], lnr[0:1, T + q * 512:T + (q + 1) * 512],
                    start=True, stop=True),
                    reads=[lnr_b, ones_row_b], writes=[pbuf[5]])
                P.step("PE", lambda e, q=q, th=th: e.matmul(
                    pb[6][:, :], B["ones_rowb"][0:1, :], rc0[0][0:1, th * TH + q * 512:th * TH + (q + 1) * 512],
                    start=True, stop=True),
                    reads=[rc0[1], B["ones_rowb_buf"]], writes=[pbuf[6]])
                P.step("DVE", lambda e: e.tensor_copy(out=B["bca"][:, :], in_=pb[4][:, :]),
                       reads=[pbuf[4]], writes=[B["bca_buf"]])
                P.step("DVE", lambda e: e.tensor_copy(out=B["bcb"][:, :], in_=pb[5][:, :]),
                       reads=[pbuf[5]], writes=[B["bcb_buf"]])
                P.step("DVE", lambda e: e.tensor_copy(out=B["bcc"][:, :], in_=pb[6][:, :]),
                       reads=[pbuf[6]], writes=[B["bcc_buf"]])
                for hc in range(NHE):
                    he = ei * NHE + hc
                    hp = B["hproc"][:, hc * 512:(hc + 1) * 512]
                    hs = abuf[:, hc * TH + q * 512: hc * TH + (q + 1) * 512]
                    P.step("DVE", lambda e, hs=hs, hp=hp: e.tensor_tensor(
                        out=hp, in0=hs, in1=B["bca"][:, :], op=AO.mult),
                        reads=[abuf_b, B["bca_buf"]], writes=[B["hproc_buf"]])
                    P.step("DVE", lambda e, hp=hp: e.tensor_tensor(
                        out=hp, in0=hp, in1=B["bcb"][:, :], op=AO.subtract),
                        reads=[B["bcb_buf"]], writes=[B["hproc_buf"]])
                    P.step("ACT", lambda e, he=he, hp=hp: e.activation(
                        out=hp, in_=hp, func=AF.Gelu,
                        scale=g1c[:, he:he + 1], bias=bb1c[:, he:he + 1]),
                        reads=[g1c_b, bb1c_b], writes=[B["hproc_buf"]])
                    P.step("DVE", lambda e, hp=hp: e.tensor_tensor(
                        out=hp, in0=hp, in1=B["bcc"][:, :], op=AO.mult),
                        reads=[B["bcc_buf"]], writes=[B["hproc_buf"]])
                def w2_load(dc, ei=ei):
                    sl = dc % 2
                    src_ap = mkap(B["l1f_w2"], (dc * 16384 + ei * 2048) * 128,
                                  [[128, 128], [128 * 128, NHE], [1, 128]])
                    P.step("QG", lambda e, sl=sl, src_ap=src_ap: e.dma_start(
                        out=wbig[sl].rearrange("p (c m) -> p c m", c=32)[:, 0:NHE, :], in_=src_ap),
                        reads=[], writes=[wbigb[sl]])
                w2_load(0)
                for dc in range(NC_DM):
                    sl = dc % 2
                    if dc + 1 < NC_DM:
                        w2_load(dc + 1)
                    r = dc % 2
                    pp, ppb = pb[7], pbuf[7]
                    def _mm2b(e, sl=sl, pp=pp, dc=dc, ei=ei, th=th, q=q):
                        last = None
                        n = NHE + (1 if ei == 7 else 0)
                        for c in range(NHE):
                            last = e.matmul(pp[:, :], wbig[sl][:, c * 128:(c + 1) * 128],
                                            B["hproc"][:, c * 512:(c + 1) * 512],
                                            start=(c == 0), stop=(c == NHE - 1 and ei != 7))
                        if ei == 7:
                            last = e.matmul(pp[:, :], b2s[0:8, dc * 128:(dc + 1) * 128],
                                            B["gatesT"][0:8, th * TH + q * 512:th * TH + (q + 1) * 512],
                                            start=False, stop=True)
                        return last
                    P.step("PE", _mm2b, reads=[wbigb[sl], B["hproc_buf"], b2s_b, B["gatesT_buf"]],
                           writes=[ppb])
                    s2 = 0
                    P.step("DVE", lambda e, s2=s2, pp=pp: e.tensor_copy(
                        out=B[f"tf{s2}"][:, :], in_=pp[:, :]),
                        reads=[ppb], writes=[B[f"tf{s2}_buf"]])
                    P.step("QG", lambda e, dc=dc, th=th, q=q, s2=s2, ei=ei: e.dma_start(
                        out=mkap(od, dc * 128 * T + th * TH + q * 512, [[T, 128], [1, 512]]),
                        in_=B[f"tf{s2}"][:, :],
                        **({} if ei == 0 else {"accum_op": AO.add})),
                        reads=[B[f"tf{s2}_buf"]] + ([] if ei == 0 else [od_b]), writes=[od_b])

    for tb in range(nblk):
        pS, pSb = pb[0], pbuf[0]
        pQ, pQb = pb[1], pbuf[1]
        for c in range(NC_DM):
            s = c % 4
            P.step("QS", lambda e, c=c, tb=tb: e.dma_start(
                out=B["tf0"][:, :],
                in_=mkap(od, c * 128 * T + tb * 512, [[T, 128], [1, 512]])),
                reads=[od_b], writes=[B["tf0_buf"]])
            zs = xln[:, c * T + tb * 512: c * T + (tb + 1) * 512]
            P.step("DVE", lambda e, zs=zs, s=s: e.tensor_tensor(
                out=at[s][:, :], in0=B["tf0"][:, :], in1=zs, op=AO.add),
                reads=[B["tf0_buf"], xln_b], writes=[atb[s]])
            sqs = sq[:, (c % 2) * 512:(c % 2) * 512 + 512]
            P.step("POOL", lambda e, s=s, sqs=sqs: e.tensor_tensor(
                out=sqs, in0=at[s][:, :], in1=at[s][:, :], op=AO.mult),
                reads=[atb[s]], writes=[sq_bufs[c % 2]])
            P.step("PE", lambda e, s=s, c=c: e.matmul(
                pS[0:1, :], B["ones_colb"][:, 0:1], at[s][:, :], start=(c == 0), stop=(c == NC_DM - 1)),
                reads=[atb[s], B["ones_colb_buf"]], writes=[pSb])
            P.step("PE", lambda e, sqs=sqs, c=c: e.matmul(
                pQ[0:1, :], ones_col[:, 0:1], sqs, start=(c == 0), stop=(c == NC_DM - 1)),
                reads=[sq_bufs[c % 2], ones_col_b], writes=[pQb])
            P.step("QS", lambda e, c=c, tb=tb, s=s: e.dma_start(
                out=mkap(vv, c * 128 * T + tb * 512, [[T, 128], [1, 512]]),
                in_=at[s][:, :]),
                reads=[atb[s]], writes=[vv_b])
        P.step("DVE", lambda e: e.tensor_scalar(
            out=srow[0:1, 0:512], in0=pS[0:1, :], scalar1=1.0 / DM, scalar2=None, op0=AO.mult),
            reads=[pSb], writes=[srow_b])
        P.step("DVE", lambda e: e.tensor_scalar(
            out=srow[0:1, 512:1024], in0=pQ[0:1, :], scalar1=1.0 / DM, scalar2=EPS, op0=AO.mult, op1=AO.add),
            reads=[pQb], writes=[srow_b])
        P.step("DVE", lambda e: e.tensor_tensor(
            out=srow[0:1, 1024:1536], in0=srow[0:1, 0:512], in1=srow[0:1, 0:512], op=AO.mult),
            reads=[srow_b], writes=[srow_b])
        P.step("DVE", lambda e: e.tensor_tensor(
            out=srow[0:1, 512:1024], in0=srow[0:1, 512:1024], in1=srow[0:1, 1024:1536], op=AO.subtract),
            reads=[srow_b], writes=[srow_b])
        P.step("ACT", lambda e: e.activation(
            out=srow[0:1, 1024:1536], in_=srow[0:1, 512:1024], func=AF.Sqrt),
            reads=[srow_b], writes=[srow_b])
        P.step("DVE", lambda e, tb=tb: e.reciprocal(
            out=lnr[0:1, tb * 512:(tb + 1) * 512], in_=srow[0:1, 1024:1536]),
            reads=[srow_b], writes=[lnr_b])
        P.step("DVE", lambda e, tb=tb: e.tensor_tensor(
            out=lnr[0:1, T + tb * 512:T + (tb + 1) * 512], in0=srow[0:1, 0:512],
            in1=lnr[0:1, tb * 512:(tb + 1) * 512], op=AO.mult),
            reads=[srow_b, lnr_b], writes=[lnr_b])
    for tb in range(nblk):
        pA, pAb = pb[2], pbuf[2]
        pB_, pBb = pb[3], pbuf[3]
        P.step("PE", lambda e, tb=tb: e.matmul(
            pA[:, :], ones_row[0:1, :], lnr[0:1, tb * 512:(tb + 1) * 512], start=True, stop=True),
            reads=[lnr_b, ones_row_b], writes=[pAb])
        P.step("PE", lambda e, tb=tb: e.matmul(
            pB_[:, :], ones_row[0:1, :], lnr[0:1, T + tb * 512:T + (tb + 1) * 512], start=True, stop=True),
            reads=[lnr_b, ones_row_b], writes=[pBb])
        for c in range(NC_DM):
            s = c % 4
            P.step("QS", lambda e, c=c, tb=tb, s=s: e.dma_start(
                out=at[s][:, :],
                in_=mkap(vv, c * 128 * T + tb * 512, [[T, 128], [1, 512]])),
                reads=[vv_b], writes=[atb[s]])
            P.step("DVE", lambda e, s=s: e.tensor_tensor(
                out=B["t512f"][:, :], in0=at[s][:, :], in1=pA[:, :], op=AO.mult),
                reads=[atb[s], pAb], writes=[B["t512f_buf"]])
            P.step("DVE", lambda e: e.tensor_tensor(
                out=B["t512f"][:, :], in0=B["t512f"][:, :], in1=pB_[:, :], op=AO.subtract),
                reads=[B["t512f_buf"], pBb], writes=[B["t512f_buf"]])
            P.step("DVE", lambda e, c=c: e.tensor_scalar(
                out=B["t512f"][:, :], in0=B["t512f"][:, :], scalar1=lngb[:, c:c + 1],
                scalar2=lngb[:, 8 + c:8 + c + 1], op0=AO.mult, op1=AO.add),
                reads=[B["t512f_buf"], lngb_b], writes=[B["t512f_buf"]])
            xs = xres[:, c * T + tb * 512: c * T + (tb + 1) * 512]
            P.step("DVE", lambda e, xs=xs: e.scalar_tensor_tensor(
                out=xs, in0=B["t512f"][:, :], scalar=0.5, in1=xs, op0=AO.mult, op1=AO.add),
                reads=[B["t512f_buf"], xres_b], writes=[xres_b])





def overlay_bufs(B):
    bl = [B["ovl32_buf"]]
    for i in range(4):
        bl += [B[f"t512a{i}_buf"], B[f"t512b{i}_buf"], B[f"ws{i}_buf"]]
    bl += [B["wbig0_buf"], B["wbig1_buf"]]
    if "hproc_buf" in B:
        bl.append(B["hproc_buf"])
    return bl



def prep_l2attn(inp):
    d = {}
    d["l2a_wq"] = np.ascontiguousarray(np.asarray(inp["L2_attn_Wq"]).T.reshape(8, 128, DM))
    d["l2a_wk"] = np.ascontiguousarray(np.asarray(inp["L2_attn_Wk"]).T.reshape(8, 128, DM))
    d["l2a_wv"] = np.ascontiguousarray(np.asarray(inp["L2_attn_Wv"]).T.reshape(8, 128, DM))
    row = np.zeros((1, 3072), np.float32)
    row[0, 0:1024] = np.asarray(inp["L2_attn_bq"])
    row[0, 1024:2048] = np.asarray(inp["L2_attn_bk"])
    row[0, 2048:3072] = np.asarray(inp["L2_attn_bv"])
    d["l2a_b"] = row
    return d


def declare_l2attn(nc, B, T, ctxs):
    names = []
    for nm in ["l2a_wq", "l2a_wk", "l2a_wv"]:
        B[nm] = nc.dram_tensor(nm, [8, 128, DM], F32, kind="ExternalInput").ap()
        names.append(nm)
    B["l2a_b"] = nc.dram_tensor("l2a_b", [1, 3072], F32, kind="ExternalInput").ap()
    names.append("l2a_b")
    for nm, shape, dt in [("l2a_b_s", [1, 3072], BF16), ("l2sco", [128, 512], F32),
                          ("l2acc0", [128, 1024], F32), ("l2acc1", [128, 1024], F32),
                          ("l2m", [128, 48], F32), ("l2scr", [128, 128], F32)]:
        cm = nc.sbuf_tensor(nm, shape, dt)
        h = cm.__enter__()
        ctxs.append(cm)
        B[nm] = h[:]
        B[nm + "_buf"] = Buf(nm)
    B["l2sco_buf2"] = Buf("l2sco_hi")
    B["l2a_out"] = nc.dram_tensor("l2a_out", [2048, DM], F32).ap()
    B["l2a_out_buf"] = Buf("l2a_out")
    return names


def emit_l2attn(P, B, T):
    assert T == 2048
    xln, xln_b = B["xln"], B["xln_buf"]
    pb, pbuf = B["psum"], B["psum_buf"]
    abuf, abuf_b = B["abuf"], B["abuf_buf"]
    ovl, ovl_b = B["ovl32"], B["ovl32_buf"]
    ovl_low = [ovl_b] + [B[f"t512a{i}_buf"] for i in range(4)] + \
              [B[f"t512b{i}_buf"] for i in range(4)] + [B[f"ws{i}_buf"] for i in range(4)]
    if "hproc_buf" in B:
        ovl_low.append(B["hproc_buf"])
    wbig = [B["wbig0"], B["wbig1"]]
    wbigb = [B["wbig0_buf"], B["wbig1_buf"]]
    bias, bias_b = B["l2a_b_s"], B["l2a_b_s_buf"]
    sco, sco_b, sco_b2 = B["l2sco"], B["l2sco_buf"], B["l2sco_buf2"]
    acc = [B["l2acc0"], B["l2acc1"]]
    accb = [B["l2acc0_buf"], B["l2acc1_buf"]]
    m8, m8b = B["l2m"], B["l2m_buf"]
    scr, scr_b = B["l2scr"], B["l2scr_buf"]
    sq, sq_bufs = B["sq"], B["sq_bufs"]
    out_d, out_db = B["l2a_out"], B["l2a_out_buf"]
    orb, orb_b = B["ones_rowb"], B["ones_rowb_buf"]
    SC = float(1.0 / np.sqrt(128.0))

    P.step("QG", lambda e: e.dma_start(out=bias, in_=B["l2a_b"]), reads=[], writes=[bias_b])

    for ch in range(2):
        for pi, (wkey, boff) in enumerate([("l2a_wq", 0), ("l2a_wk", 1024), ("l2a_wv", 2048)]):
            for dh in range(2):
                sl = (pi * 2 + dh) % 2
                src_ap = mkap(B[wkey], dh * 512, [[DM, 128], [128 * DM, NC_DM], [1, 512]])
                P.step("QG", lambda e, sl=sl, src_ap=src_ap: e.dma_start(
                    out=wbig[sl].rearrange("p (c m) -> p c m", c=8), in_=src_ap),
                    reads=[], writes=[wbigb[sl], ovl_b])
                for s in range(8):
                    tt = 2 * s + ch
                    pp, ppb = pb[4 + (s % 2)], pbuf[4 + (s % 2)]
                    def _mm(e, tt=tt, sl=sl, pp=pp, boff=boff, dh=dh):
                        last = None
                        for c in range(NC_DM):
                            last = e.matmul(pp[:, :], xln[:, c * T + tt * 128:c * T + (tt + 1) * 128],
                                            wbig[sl][:, c * 512:(c + 1) * 512],
                                            start=(c == 0), stop=False)
                        last = e.matmul(pp[:, :], orb[0:1, :],
                                        bias[0:1, boff + dh * 512: boff + (dh + 1) * 512],
                                        start=False, stop=True)
                        return last
                    P.step("PE", _mm, reads=[xln_b, wbigb[sl], bias_b, orb_b], writes=[ppb])
                    if pi == 0:
                        P.step("DVE", lambda e, s=s, dh=dh, pp=pp: e.tensor_scalar(
                            out=abuf[:, s * 1024 + dh * 512: s * 1024 + (dh + 1) * 512],
                            in0=pp[:, :], scalar1=SC, scalar2=None, op0=AO.mult),
                            reads=[ppb], writes=[abuf_b])
                    elif pi == 1:
                        P.step("DVE", lambda e, s=s, dh=dh, pp=pp: e.tensor_copy(
                            out=abuf[:, 8192 + s * 1024 + dh * 512: 8192 + s * 1024 + (dh + 1) * 512],
                            in_=pp[:, :]),
                            reads=[ppb], writes=[abuf_b])
                    else:
                        P.step("DVE", lambda e, s=s, dh=dh, pp=pp: e.tensor_copy(
                            out=ovl[:, s * 1024 + dh * 512: s * 1024 + (dh + 1) * 512],
                            in_=pp[:, :]),
                            reads=[ppb], writes=ovl_low)
        for h in range(8):
            scob = sco_b if h < 4 else sco_b2
            a_i = 0 if h < 4 else 1
            base = h * 64
            for sqi in range(8):
                for ski in range(8):
                    qs = abuf[:, sqi * 1024 + h * 128: sqi * 1024 + (h + 1) * 128]
                    ks = abuf[:, 8192 + ski * 1024 + h * 128: 8192 + ski * 1024 + (h + 1) * 128]
                    col = sco[:, base + sqi * 8 + ski: base + sqi * 8 + ski + 1]
                    prod = acc[a_i][:, ski * 128:(ski + 1) * 128]
                    P.step("POOL", lambda e, qs=qs, ks=ks, prod=prod: e.tensor_tensor(
                        out=prod, in0=qs, in1=ks, op=AO.mult),
                        reads=[abuf_b], writes=[accb[a_i]])
                    P.step("DVE", lambda e, col=col, prod=prod: e.tensor_reduce(
                        out=col, in_=prod, axis=mybir.AxisListType.X, op=AO.add),
                        reads=[accb[a_i]], writes=[scob])
            for sqi in range(8):
                P.step("DVE", lambda e, sqi=sqi: e.tensor_reduce(
                    out=m8[:, sqi:sqi + 1], in_=sco[:, base + sqi * 8: base + sqi * 8 + 8],
                    axis=mybir.AxisListType.X, op=AO.max),
                    reads=[scob], writes=[m8b])
            P.step("DVE", lambda e: e.tensor_scalar(
                out=m8[:, 0:8], in0=m8[:, 0:8], scalar1=0.0, scalar2=None, op0=AO.max),
                reads=[m8b], writes=[m8b])
            mbc = mkap(m8, 0, [[48, 128], [1, 8], [0, 8]])
            P.step("DVE", lambda e, h=h, mbc=mbc: e.tensor_tensor(
                out=sco[:, base:base + 64], in0=sco[:, base:base + 64], in1=mbc, op=AO.subtract),
                reads=[m8b, scob], writes=[scob])
            P.step("ACT", lambda e, h=h: e.activation(
                out=sco[:, base:base + 64], in_=sco[:, base:base + 64], func=AF.Exp),
                reads=[scob], writes=[scob])
            for sqi in range(8):
                P.step("DVE", lambda e, sqi=sqi: e.tensor_reduce(
                    out=m8[:, 8 + sqi:9 + sqi], in_=sco[:, base + sqi * 8: base + sqi * 8 + 8],
                    axis=mybir.AxisListType.X, op=AO.add),
                    reads=[scob], writes=[m8b])
            P.step("DVE", lambda e: e.tensor_scalar(
                out=m8[:, 16:24], in0=m8[:, 0:8], scalar1=-1.0, scalar2=None, op0=AO.mult),
                reads=[m8b], writes=[m8b])
            P.step("ACT", lambda e: e.activation(
                out=m8[:, 16:24], in_=m8[:, 16:24], func=AF.Exp),
                reads=[m8b], writes=[m8b])
            P.step("DVE", lambda e: e.scalar_tensor_tensor(
                out=m8[:, 8:16], in0=m8[:, 16:24], scalar=505.0, in1=m8[:, 8:16],
                op0=AO.mult, op1=AO.add),
                reads=[m8b], writes=[m8b])
            P.step("DVE", lambda e: e.reciprocal(out=m8[:, 24:32], in_=m8[:, 8:16]),
                   reads=[m8b], writes=[m8b])
            for sqi in range(8):
                for ski in range(8):
                    vs = ovl[:, ski * 1024 + h * 128: ski * 1024 + (h + 1) * 128]
                    pcol = sco[:, base + sqi * 8 + ski: base + sqi * 8 + ski + 1]
                    dst = acc[a_i][:, sqi * 128:(sqi + 1) * 128]
                    if ski == 0:
                        P.step("DVE", lambda e, vs=vs, pcol=pcol, dst=dst: e.tensor_scalar(
                            out=dst, in0=vs, scalar1=pcol, scalar2=None, op0=AO.mult),
                            reads=[ovl_b, scob], writes=[accb[a_i]])
                    else:
                        P.step("DVE", lambda e, vs=vs, pcol=pcol, dst=dst: e.scalar_tensor_tensor(
                            out=dst, in0=vs, scalar=pcol, in1=dst, op0=AO.mult, op1=AO.add),
                            reads=[ovl_b, scob], writes=[accb[a_i]])
            for sqi in range(8):
                dst = acc[a_i][:, sqi * 128:(sqi + 1) * 128]
                P.step("DVE", lambda e, dst=dst, sqi=sqi: e.tensor_scalar(
                    out=dst, in0=dst, scalar1=m8[:, 24 + sqi:24 + sqi + 1], scalar2=None, op0=AO.mult),
                    reads=[m8b, accb[a_i]], writes=[accb[a_i]])
            P.step("QS", lambda e, ch=ch, h=h, a_i=a_i: e.dma_start(
                out=mkap(out_d, ch * 128 * DM + h * 128, [[DM, 128], [256 * DM, 8], [1, 128]]),
                in_=mkap(acc[a_i], 0, [[1024, 128], [128, 8], [1, 128]])),
                reads=[accb[a_i]], writes=[out_db])





def prep_l3attn(inp):
    d = {}
    d["l3a_win"] = tile_lhs(np.ascontiguousarray(np.asarray(inp["L3_attn_W_in"]).T))
    d["l3a_bin"] = col_tiles(np.asarray(inp["L3_attn_b_in"]))
    d["l3a_wxp"] = np.ascontiguousarray(
        np.asarray(inp["L3_attn_W_xp"]).T.reshape(8, 128, 64).transpose(1, 0, 2).reshape(128, 512))
    d["l3a_bxp"] = np.ascontiguousarray(np.asarray(inp["L3_attn_b_xp"]).reshape(64, 1))
    d["l3a_alog"] = np.ascontiguousarray(np.asarray(inp["L3_attn_A_log"]).reshape(64, 1))
    d["l3a_dmat"] = np.ascontiguousarray(np.asarray(inp["L3_attn_Dmat"]))
    d["l3a_wout"] = tile_lhs(np.ascontiguousarray(np.asarray(inp["L3_attn_W_out"]).T))
    d["l3a_bout"] = col_tiles(np.asarray(inp["L3_attn_b_out"]))
    gb = np.zeros((128, 16), np.float32)
    gb[:, 0:8] = col_tiles(inp["L3_attn_gamma"])
    gb[:, 8:16] = col_tiles(inp["L3_attn_beta"])
    d["l3a_gb"] = gb
    return d


def declare_l3attn(nc, B, T, ctxs):
    names = []
    for nm, shape in [("l3a_win", [16, DM, 128]), ("l3a_bin", [128, 16]),
                      ("l3a_wxp", [128, 512]), ("l3a_bxp", [64, 1]),
                      ("l3a_alog", [64, 1]), ("l3a_dmat", [64, DM]),
                      ("l3a_wout", [8, DM, 128]), ("l3a_bout", [128, 8]),
                      ("l3a_gb", [128, 16])]:
        B[nm] = nc.dram_tensor(nm, shape, F32, kind="ExternalInput").ap()
        names.append(nm)
    ws0a = nc.lookup_mloc("ws0").addr
    ws3a = nc.lookup_mloc("ws3").addr
    assert ws3a + 2048 - ws0a == 8192
    hB = nc.alloc_sbuf_tensor_at("scanB", [64, T], F32, offset=ws0a)
    B["scanB"] = hB[:]
    B["scanB_buf"] = Buf("scanB")
    for nm, shape, sdt in [("l3a_bin_s", [128, 16], F32), ("l3a_wxp_s", [128, 512], BF16),
                           ("l3a_misc", [64, 16], F32), ("l3a_dmat_s", [64, DM], F32),
                           ("l3a_bout_s", [128, 8], F32), ("l3a_gb_s", [128, 16], F32),
                           ("scanA", [128, T], F32)]:
        cm = nc.sbuf_tensor(nm, shape, sdt)
        h = cm.__enter__()
        ctxs.append(cm)
        B[nm] = h[:]
        B[nm + "_buf"] = Buf(nm)
    B["l3a_y"] = nc.dram_tensor("l3a_y", [DM, T], F32).ap()
    B["l3a_y_buf"] = Buf("l3a_y")
    return names


def emit_ln_affine(P, B, T, src, src_buf, dst, dst_buf, gcol, bcol, gb_buf):
    ones_colb, ones_colb_b = B["ones_colb"], B["ones_colb_buf"]
    ones_row, ones_row_b = B["ones_row"], B["ones_row_buf"]
    sq, sq_bufs = B["sq"], B["sq_bufs"]
    srow, srow_b = B["srow"], B["srow_buf"]
    pb, pbuf = B["psum"], B["psum_buf"]
    tmp, tmp_b = B["t512f"], B["t512f_buf"]
    inv = 1.0 / DM
    for tb in range(T // 512):
        pS, pSb = pb[0][0:1, :], pbuf[0]
        pQ, pQb = pb[1][0:1, :], pbuf[1]
        for c in range(NC_DM):
            xs = src[:, c * T + tb * 512: c * T + (tb + 1) * 512]
            sqs = sq[:, (c % 2) * 512:(c % 2) * 512 + 512]
            sqb = sq_bufs[c % 2]
            P.step("POOL", lambda e, xs=xs, sqs=sqs: e.tensor_tensor(
                out=sqs, in0=xs, in1=xs, op=AO.mult),
                reads=[src_buf], writes=[sqb])
            P.step("PE", lambda e, xs=xs, c=c: e.matmul(
                pS, ones_colb[:, 0:1], xs, start=(c == 0), stop=(c == NC_DM - 1)),
                reads=[src_buf, ones_colb_b], writes=[pSb])
            P.step("PE", lambda e, sqs=sqs, c=c: e.matmul(
                pQ, B["ones_col"][:, 0:1], sqs, start=(c == 0), stop=(c == NC_DM - 1)),
                reads=[sqb, B["ones_col_buf"]], writes=[pQb])
        P.step("DVE", lambda e: e.tensor_scalar(
            out=srow[0:1, 0:512], in0=pS, scalar1=inv, scalar2=None, op0=AO.mult),
            reads=[pSb], writes=[srow_b])
        P.step("DVE", lambda e: e.tensor_scalar(
            out=srow[0:1, 512:1024], in0=pQ, scalar1=inv, scalar2=EPS, op0=AO.mult, op1=AO.add),
            reads=[pQb], writes=[srow_b])
        P.step("DVE", lambda e: e.tensor_tensor(
            out=srow[0:1, 1024:1536], in0=srow[0:1, 0:512], in1=srow[0:1, 0:512], op=AO.mult),
            reads=[srow_b], writes=[srow_b])
        P.step("DVE", lambda e: e.tensor_tensor(
            out=srow[0:1, 512:1024], in0=srow[0:1, 512:1024], in1=srow[0:1, 1024:1536], op=AO.subtract),
            reads=[srow_b], writes=[srow_b])
        P.step("ACT", lambda e: e.activation(
            out=srow[0:1, 1024:1536], in_=srow[0:1, 512:1024], func=AF.Sqrt),
            reads=[srow_b], writes=[srow_b])
        P.step("DVE", lambda e: e.reciprocal(
            out=srow[0:1, 512:1024], in_=srow[0:1, 1024:1536]),
            reads=[srow_b], writes=[srow_b])
        P.step("DVE", lambda e: e.tensor_tensor(
            out=srow[0:1, 1024:1536], in0=srow[0:1, 0:512], in1=srow[0:1, 512:1024], op=AO.mult),
            reads=[srow_b], writes=[srow_b])
        pA, pAb = pb[2], pbuf[2]
        pB_, pBb = pb[3], pbuf[3]
        P.step("PE", lambda e: e.matmul(
            pA[:, :], ones_row[0:1, :], srow[0:1, 512:1024], start=True, stop=True),
            reads=[srow_b, ones_row_b], writes=[pAb])
        P.step("PE", lambda e: e.matmul(
            pB_[:, :], ones_row[0:1, :], srow[0:1, 1024:1536], start=True, stop=True),
            reads=[srow_b, ones_row_b], writes=[pBb])
        for c in range(NC_DM):
            xs = src[:, c * T + tb * 512: c * T + (tb + 1) * 512]
            ds = dst[:, c * T + tb * 512: c * T + (tb + 1) * 512]
            P.step("DVE", lambda e, xs=xs: e.tensor_tensor(
                out=tmp[:, :], in0=xs, in1=pA[:, :], op=AO.mult),
                reads=[src_buf, pAb], writes=[tmp_b])
            P.step("DVE", lambda e: e.tensor_tensor(
                out=tmp[:, :], in0=tmp[:, :], in1=pB_[:, :], op=AO.subtract),
                reads=[tmp_b, pBb], writes=[tmp_b])
            P.step("DVE", lambda e, ds=ds, c=c: e.tensor_scalar(
                out=ds, in0=tmp[:, :], scalar1=gcol[:, c:c + 1], scalar2=bcol[:, c:c + 1],
                op0=AO.mult, op1=AO.add),
                reads=[tmp_b, gb_buf], writes=[dst_buf])


def emit_l3attn(P, B, T, _stop=99):
    import os
    _stop = int(os.environ.get("L3A_STOP", _stop))
    xln, xln_b = B["xln"], B["xln_buf"]
    xres, xres_b = B["xres"], B["xres_buf"]
    pb, pbuf = B["psum"], B["psum_buf"]
    abuf, abuf_b = B["abuf"], B["abuf_buf"]
    scr0, scr0_b = B["scr0"], B["scr0_buf"]
    ws = [B[f"ws{i}"] for i in range(4)]
    wsb = [B[f"ws{i}_buf"] for i in range(4)]
    at = [B[f"t512a{i}"] for i in range(4)]
    atb = [B[f"t512a{i}_buf"] for i in range(4)]
    t1 = [B[f"t512b{i}"] for i in range(4)]
    t1b = [B[f"t512b{i}_buf"] for i in range(4)]
    gb_s, gb_sb = B["l3a_gb_s"], B["l3a_gb_s_buf"]
    bin_s, bin_sb = B["l3a_bin_s"], B["l3a_bin_s_buf"]
    wxp, wxp_b = B["l3a_wxp_s"], B["l3a_wxp_s_buf"]
    misc, misc_b = B["l3a_misc"], B["l3a_misc_buf"]
    dmat, dmat_b = B["l3a_dmat_s"], B["l3a_dmat_s_buf"]
    bout, bout_b = B["l3a_bout_s"], B["l3a_bout_s_buf"]
    sA, sA_b = B["scanA"], B["scanA_buf"]
    sB_, sB_b = B["scanB"], B["scanB_buf"]
    yd, yd_b = B["l3a_y"], B["l3a_y_buf"]
    nblk = T // 512

    for nm, key, kb in [("l3a_bin", bin_s, bin_sb), ("l3a_dmat", dmat, dmat_b),
                        ("l3a_bout", bout, bout_b), ("l3a_gb", gb_s, gb_sb),
                        ("l3a_bxp", misc, misc_b)]:
        def _ld(e, key=key, nm=nm):
            if nm == "l3a_bxp":
                return e.dma_start(out=key[0:64, 0:1], in_=B[nm])
            if nm == "l3a_dmat":
                return e.dma_start(out=key[0:64, :], in_=B[nm])
            return e.dma_start(out=key, in_=B[nm])
        P.step("QS", _ld, reads=[], writes=[kb])
    P.step("QG", lambda e: e.dma_start(out=wxp, in_=B["l3a_wxp"]), reads=[], writes=[wxp_b])
    P.step("QS", lambda e: e.dma_start(out=misc[0:64, 1:2], in_=B["l3a_alog"]),
           reads=[], writes=[misc_b])

    emit_ln_affine(P, B, T, xln, xln_b, abuf, abuf_b, gb_s[:, 0:8], gb_s[:, 8:16], gb_sb)
    if _stop <= 1:
        return

    def win_load(oc):
        sl = oc % 4
        src_ap = mkap(B["l3a_win"], oc * DM * 128, [[128, 128], [128 * 128, NC_DM], [1, 128]])
        P.step("QG", lambda e, sl=sl, src_ap=src_ap: e.dma_start(
            out=ws[sl].rearrange("p (c m) -> p c m", c=NC_DM), in_=src_ap),
            reads=[], writes=[wsb[sl]])
    win_load(0); win_load(1)
    for oc in range(16):
        sl = oc % 4
        if oc + 2 < 16:
            win_load(oc + 2)
        for tb in range(nblk):
            r = tb % 2
            pp, ppb = pb[4 + r], pbuf[4 + r]
            def _mm(e, sl=sl, tb=tb, pp=pp):
                last = None
                for c in range(NC_DM):
                    last = e.matmul(pp[:, :], ws[sl][:, c * 128:(c + 1) * 128],
                                    abuf[:, c * T + tb * 512:c * T + (tb + 1) * 512],
                                    start=(c == 0), stop=(c == NC_DM - 1))
                return last
            P.step("PE", _mm, reads=[wsb[sl], abuf_b], writes=[ppb])
            s = (oc * nblk + tb) % 4
            func = AF.Silu if oc < 8 else AF.Sigmoid
            rowbase = (1024 + oc * 128) if oc < 8 else ((oc - 8) * 128)
            P.step("ACT", lambda e, oc=oc, s=s, pp=pp, func=func: e.activation(
                out=at[s][:, :], in_=pp[:, :], func=func, bias=bin_s[:, oc:oc + 1]),
                reads=[ppb, bin_sb], writes=[atb[s]])
            P.step("QS", lambda e, rowbase=rowbase, tb=tb, s=s: e.dma_start(
                out=mkap(scr0, rowbase * T + tb * 512, [[T, 128], [1, 512]]),
                in_=at[s][:, :]),
                reads=[atb[s]], writes=[scr0_b])

    if _stop <= 2:
        return
    for tb in range(nblk):
        pp, ppb = pb[6], pbuf[6]
        for c in range(NC_DM):
            s = c % 4
            P.step("QS", lambda e, c=c, tb=tb, s=s: e.dma_start(
                out=t1[s][:, :],
                in_=mkap(scr0, (1024 + c * 128) * T + tb * 512, [[T, 128], [1, 512]])),
                reads=[scr0_b], writes=[t1b[s]])
            P.step("PE", lambda e, c=c, s=s, pp=pp: e.matmul(
                pp[0:64, :], wxp[:, c * 64:(c + 1) * 64], t1[s][:, :],
                start=(c == 0), stop=(c == NC_DM - 1)),
                reads=[t1b[s], wxp_b], writes=[ppb])
        P.step("DVE", lambda e, tb=tb, pp=pp: e.tensor_scalar(
            out=sA[0:64, tb * 512:(tb + 1) * 512], in0=pp[0:64, :],
            scalar1=misc[0:64, 0:1], scalar2=None, op0=AO.add),
            reads=[ppb, misc_b], writes=[sA_b])

    if _stop <= 3:
        return
    P.step("ACT", lambda e: e.activation(out=misc[0:64, 2:3], in_=misc[0:64, 1:2], func=AF.Exp),
           reads=[misc_b], writes=[misc_b])
    P.step("DVE", lambda e: e.tensor_scalar(
        out=misc[0:64, 2:3], in0=misc[0:64, 2:3], scalar1=-1.0, scalar2=None, op0=AO.mult),
        reads=[misc_b], writes=[misc_b])
    P.step("ACT", lambda e: e.activation(out=misc[0:64, 4:5], in_=misc[0:64, 2:3], func=AF.Exp),
           reads=[misc_b], writes=[misc_b])
    for k in range(1, 11):
        P.step("DVE", lambda e, k=k: e.tensor_tensor(
            out=misc[0:64, 4 + k:5 + k], in0=misc[0:64, 3 + k:4 + k],
            in1=misc[0:64, 3 + k:4 + k], op=AO.mult),
            reads=[misc_b], writes=[misc_b])

    if _stop <= 4:
        return
    cur, curb = sA[0:64, 0:T], sA_b
    bufs = [(sB_, sB_b, 0), (sA, sA_b, 0)]
    wscpl = [wsb[i] for i in range(4)]
    for k in range(11):
        s = 1 << k
        dstt, dstb, dro = bufs[k % 2]
        dst = dstt[dro:dro + 64, 0:T]
        a2k = misc[0:64, 4 + k:5 + k]
        wl = [dstb] + (wscpl if (k % 2) == 0 else [])
        P.step("DVE", lambda e, dst=dst, cur=cur, s=s, a2k=a2k: e.scalar_tensor_tensor(
            out=dst[0:64, s:T], in0=cur[0:64, 0:T - s], scalar=a2k,
            in1=cur[0:64, s:T], op0=AO.mult, op1=AO.add),
            reads=[curb, misc_b], writes=wl)
        P.step("DVE", lambda e, dst=dst, cur=cur, s=s: e.tensor_copy(
            out=dst[0:64, 0:s], in_=cur[0:64, 0:s]),
            reads=[curb], writes=wl)
        cur, curb = dst, dstb

    if _stop <= 5:
        return
    for mc in range(NC_DM):
        for tb in range(nblk):
            r = tb % 2
            pp, ppb = pb[4 + r], pbuf[4 + r]
            P.step("PE", lambda e, mc=mc, tb=tb, pp=pp: e.matmul(
                pp[:, :], dmat[0:64, mc * 128:(mc + 1) * 128], cur[0:64, tb * 512:(tb + 1) * 512],
                start=True, stop=True),
                reads=[dmat_b, curb], writes=[ppb])
            s = (mc * nblk + tb) % 4
            P.step("QS", lambda e, mc=mc, tb=tb, s=s: e.dma_start(
                out=t1[s][:, :],
                in_=mkap(scr0, (mc * 128) * T + tb * 512, [[T, 128], [1, 512]])),
                reads=[scr0_b], writes=[t1b[s]])
            P.step("DVE", lambda e, mc=mc, tb=tb, s=s, pp=pp: e.tensor_tensor(
                out=abuf[:, mc * T + tb * 512:mc * T + (tb + 1) * 512],
                in0=pp[:, :], in1=t1[s][:, :], op=AO.mult),
                reads=[ppb, t1b[s]], writes=[abuf_b])

    if _stop <= 6:
        return
    def wout_load(oc):
        sl = oc % 4
        src_ap = mkap(B["l3a_wout"], oc * DM * 128, [[128, 128], [128 * 128, NC_DM], [1, 128]])
        P.step("QG", lambda e, sl=sl, src_ap=src_ap: e.dma_start(
            out=ws[sl].rearrange("p (c m) -> p c m", c=NC_DM), in_=src_ap),
            reads=[], writes=[wsb[sl], sB_b])
    wout_load(0); wout_load(1)
    for oc in range(NC_DM):
        sl = oc % 4
        if oc + 2 < NC_DM:
            wout_load(oc + 2)
        for tb in range(nblk):
            r = tb % 2
            pp, ppb = pb[4 + r], pbuf[4 + r]
            def _mm(e, sl=sl, tb=tb, pp=pp):
                last = None
                for c in range(NC_DM):
                    last = e.matmul(pp[:, :], ws[sl][:, c * 128:(c + 1) * 128],
                                    abuf[:, c * T + tb * 512:c * T + (tb + 1) * 512],
                                    start=(c == 0), stop=(c == NC_DM - 1))
                return last
            P.step("PE", _mm, reads=[wsb[sl], abuf_b], writes=[ppb])
            P.step("DVE", lambda e, oc=oc, pp=pp: e.tensor_scalar(
                out=B["t512f"][:, :], in0=pp[:, :], scalar1=bout[:, oc:oc + 1], scalar2=None, op0=AO.add),
                reads=[ppb, bout_b], writes=[B["t512f_buf"]])
            P.step("QS", lambda e, oc=oc, tb=tb: e.dma_start(
                out=mkap(yd, oc * 128 * T + tb * 512, [[T, 128], [1, 512]]),
                in_=B["t512f"][:, :]),
                reads=[B["t512f_buf"]], writes=[yd_b])

    for c in range(NC_DM):
        for tb in range(nblk):
            xs = xres[:, c * T + tb * 512: c * T + (tb + 1) * 512]
            zs = xln[:, c * T + tb * 512: c * T + (tb + 1) * 512]
            P.step("DVE", lambda e, xs=xs, zs=zs: e.tensor_tensor(
                out=xs, in0=xs, in1=zs, op=AO.add),
                reads=[xln_b, xres_b], writes=[xres_b])





def _sinusoidT(S, D):
    pos = np.arange(S, dtype=np.float64)
    idx = np.arange(D // 2, dtype=np.float64)
    inv_freq = 1.0 / 10000.0 ** (2.0 * idx / D)
    ang = pos[:, None] * inv_freq[None, :]
    tab = np.concatenate([np.sin(ang), np.cos(ang)], axis=-1).astype(np.float32)
    return np.ascontiguousarray(tab.T)


def prep_l1attn(inp, T=2048):
    d = {}
    d["l1a_wq"] = tile_lhs(np.ascontiguousarray(np.asarray(inp["L1_attn_Wq"])))
    d["l1a_wk"] = tile_lhs(np.ascontiguousarray(np.asarray(inp["L1_attn_Wk"])))
    d["l1a_wv"] = np.ascontiguousarray(np.asarray(inp["L1_attn_Wv"]).reshape(8, 128, DM))
    d["l1a_wp"] = tile_lhs(np.ascontiguousarray(np.asarray(inp["L1_attn_Wp"])))
    d["l1a_wo"] = tile_lhs(np.ascontiguousarray(np.asarray(inp["L1_attn_Wo"])))
    cols = np.zeros((128, 40), np.float32)
    cols[:, 0:8] = col_tiles(inp["L1_attn_bq"])
    cols[:, 8:16] = col_tiles(inp["L1_attn_bk"])
    cols[:, 16:24] = col_tiles(np.asarray(inp["L1_attn_u"]).reshape(-1))
    cols[:, 24:32] = col_tiles(np.asarray(inp["L1_attn_v"]).reshape(-1))
    cols[:, 32:40] = col_tiles(inp["L1_attn_bo"])
    d["l1a_cols"] = cols
    d["l1a_bv"] = np.asarray(inp["L1_attn_bv"]).reshape(1, DM)
    d["l1a_bp"] = col_tiles(inp["L1_attn_bp"])
    d["l1a_sin"] = _sinusoidT(T, DM)
    return d


def declare_l1attn(nc, B, T, ctxs):
    names = []
    for nm, shape in [("l1a_wq", [8, DM, 128]), ("l1a_wk", [8, DM, 128]),
                      ("l1a_wv", [8, 128, DM]), ("l1a_wp", [8, DM, 128]),
                      ("l1a_wo", [8, DM, 128]), ("l1a_cols", [128, 40]),
                      ("l1a_bv", [1, DM]), ("l1a_bp", [128, 8]), ("l1a_sin", [DM, T])]:
        B[nm] = nc.dram_tensor(nm, shape, F32, kind="ExternalInput").ap()
        names.append(nm)
    for nm, shape, sdt in [("l1a_cols_s", [128, 40], F32), ("l1a_bv_s", [1, DM], BF16),
                           ("l1a_bp_s", [128, 8], F32),
                           ("l1a_qh", [64, T], BF16), ("l1a_kh", [64, T], BF16),
                           ("l1a_vt", [128, 130], BF16),
                           ("l1a_sh", [128, 512], F32), ("l1a_pt", [128, 512], BF16),
                           ("l1a_av", [128, 512], F32), ("ones_sq", [128, 16], BF16),
                           ("zero16", [128, 16], BF16), ("ones_sqf", [128, 128], F32)]:
        cm = nc.sbuf_tensor(nm, shape, sdt)
        h = cm.__enter__()
        ctxs.append(cm)
        B[nm] = h[:]
        B[nm + "_buf"] = Buf(nm)
    for nm, shape, dt in [("l1a_qvd", [DM, T], BF16), ("l1a_qud", [DM, T], BF16),
                          ("l1a_kd", [DM, T], BF16), ("l1a_vtd", [T, 1040], BF16),
                          ("l1a_posd", [DM, T], BF16), ("l1a_aod", [DM, T], BF16),
                          ("l1a_pp0", [T * (T + 1)], BF16), ("l1a_pp1", [T * (T + 1)], BF16)]:
        B[nm] = nc.dram_tensor(nm, shape, dt).ap()
        B[nm + "_buf"] = Buf(nm)
    return names


def emit_l1attn(P, B, T):
    xln, xln_b = B["xln"], B["xln_buf"]
    xres, xres_b = B["xres"], B["xres_buf"]
    pb, pbuf = B["psum"], B["psum_buf"]
    abuf, abuf_b = B["abuf"], B["abuf_buf"]
    ws = [B[f"ws{i}"] for i in range(4)]
    wsb = [B[f"ws{i}_buf"] for i in range(4)]
    at = [B[f"t512a{i}"] for i in range(4)]
    atb = [B[f"t512a{i}_buf"] for i in range(4)]
    t1 = [B[f"t512b{i}"] for i in range(4)]
    t1b = [B[f"t512b{i}_buf"] for i in range(4)]
    wbig = [B["wbig0"], B["wbig1"]]
    wbigb = [B["wbig0_buf"], B["wbig1_buf"]]
    cols, cols_b = B["l1a_cols_s"], B["l1a_cols_s_buf"]
    bvr, bvr_b = B["l1a_bv_s"], B["l1a_bv_s_buf"]
    qh, qh_b = B["l1a_qh"], B["l1a_qh_buf"]
    kh, kh_b = B["l1a_kh"], B["l1a_kh_buf"]
    vt, vt_b = B["l1a_vt"], B["l1a_vt_buf"]
    ph, ph_b = B["l1a_kh"], B["l1a_kh_buf"]
    sh, sh_b = B["l1a_sh"], B["l1a_sh_buf"]
    pt, pt_b = B["l1a_pt"], B["l1a_pt_buf"]
    av, av_b = B["l1a_av"], B["l1a_av_buf"]
    osq, osq_b = B["ones_sq"], B["ones_sq_buf"]
    idn, idn_b = B["idn128"], B["idn128_buf"]
    cc, cc_b = B["cconst"], B["cconst_buf"]
    nblk = T // 512
    NT = T // 128

    P.step("QS", lambda e: e.dma_start(out=cols, in_=B["l1a_cols"]), reads=[], writes=[cols_b])
    P.step("QS", lambda e: e.dma_start(out=B["l1a_bp_s"], in_=B["l1a_bp"]),
           reads=[], writes=[B["l1a_bp_s_buf"]])
    P.step("DVE", lambda e: e.memset(B["zero16"][:, :], 0.0), writes=[B["zero16_buf"]])
    P.step("DVE", lambda e: e.memset(B["ones_sqf"][:, :], 1.0), writes=[B["ones_sqf_buf"]])
    P.step("QG", lambda e: e.dma_start(out=bvr, in_=B["l1a_bv"]), reads=[], writes=[bvr_b])
    P.step("DVE", lambda e: e.memset(osq[:, :], 1.0), writes=[osq_b])
    P.step("DVE", lambda e: e.tensor_tensor(
        out=cols[:, 16:24], in0=cols[:, 16:24], in1=cols[:, 0:8], op=AO.add),
        reads=[cols_b], writes=[cols_b])
    P.step("DVE", lambda e: e.tensor_tensor(
        out=cols[:, 24:32], in0=cols[:, 24:32], in1=cols[:, 0:8], op=AO.add),
        reads=[cols_b], writes=[cols_b])

    P.step("DVE", lambda e: e.memset(abuf[:, 0:8196], 0.0), writes=[abuf_b])
    for pkey in ("l1a_pp0", "l1a_pp1"):
        P.step("QS", lambda e, pkey=pkey: e.dma_start(
            out=mkap(B[pkey], 0, [[32784, 128], [8196, 4], [1, 8196]]),
            in_=mkap(abuf, 0, [[16384, 128], [0, 4], [1, 8196]])),
            reads=[abuf_b], writes=[B[pkey + "_buf"]])

    def fm_proj(wkey, bias_cols, dst, dst_b, src, src_b):
        def wload(oc):
            sl = oc % 4
            src_ap = mkap(B[wkey], oc * DM * 128, [[128, 128], [128 * 128, NC_DM], [1, 128]])
            P.step("QG", lambda e, sl=sl, src_ap=src_ap: e.dma_start(
                out=ws[sl].rearrange("p (c m) -> p c m", c=NC_DM), in_=src_ap),
                reads=[], writes=[wsb[sl]])
        wload(0); wload(1)
        for oc in range(NC_DM):
            sl = oc % 4
            if oc + 2 < NC_DM:
                wload(oc + 2)
            for tb in range(nblk):
                r = tb % 2
                pp, ppb = pb[4 + r], pbuf[4 + r]
                def _mm(e, sl=sl, tb=tb, pp=pp):
                    last = None
                    for c in range(NC_DM):
                        last = e.matmul(pp[:, :], ws[sl][:, c * 128:(c + 1) * 128],
                                        src[:, c * T + tb * 512:c * T + (tb + 1) * 512],
                                        start=(c == 0), stop=(c == NC_DM - 1))
                    return last
                P.step("PE", _mm, reads=[wsb[sl], src_b], writes=[ppb])
                s = (oc * nblk + tb) % 4
                P.step("DVE", lambda e, oc=oc, s=s, pp=pp, bias_cols=bias_cols: e.tensor_scalar(
                    out=at[s][:, :], in0=pp[:, :], scalar1=bias_cols[:, oc:oc + 1],
                    scalar2=None, op0=AO.add),
                    reads=[ppb, cols_b, B["l1a_bp_s_buf"]], writes=[atb[s]])
                P.step("QS", lambda e, oc=oc, tb=tb, s=s, dst=dst: e.dma_start(
                    out=mkap(dst, oc * 128 * T + tb * 512, [[T, 128], [1, 512]]),
                    in_=at[s][:, :]),
                    reads=[atb[s]], writes=[dst_b])

    fm_proj("l1a_wq", cols[:, 16:24], B["l1a_qud"], B["l1a_qud_buf"], xln, xln_b)
    fm_proj("l1a_wq", cols[:, 24:32], B["l1a_qvd"], B["l1a_qvd_buf"], xln, xln_b)
    fm_proj("l1a_wk", cols[:, 8:16], B["l1a_kd"], B["l1a_kd_buf"], xln, xln_b)

    P.step("QG", lambda e: e.dma_start(
        out=abuf.rearrange("p (c t) -> p c t", c=NC_DM),
        in_=mkap(B["l1a_sin"], 0, [[T, 128], [128 * T, NC_DM], [1, T]])),
        reads=[], writes=[abuf_b])
    fm_proj("l1a_wp", B["l1a_bp_s"][:, 0:8], B["l1a_posd"],
            B["l1a_posd_buf"], abuf, abuf_b)

    P.step("DVE", lambda e: e.memset(vt[:, 64:65], 1.0), writes=[vt_b])
    P.step("DVE", lambda e: e.memset(vt[:, 129:130], 1.0), writes=[vt_b])
    for dh in range(2):
        sl = dh % 2
        src_ap = mkap(B["l1a_wv"], dh * 512, [[DM, 128], [128 * DM, NC_DM], [1, 512]])
        P.step("QG", lambda e, sl=sl, src_ap=src_ap: e.dma_start(
            out=wbig[sl].rearrange("p (c m) -> p c m", c=8), in_=src_ap),
            reads=[], writes=[wbigb[sl]])
        for tile in range(NT):
            pp, ppb = pb[4 + (tile % 2)], pbuf[4 + (tile % 2)]
            def _mmv(e, tile=tile, sl=sl, pp=pp, dh=dh):
                last = None
                for c in range(NC_DM):
                    last = e.matmul(pp[:, :], xln[:, c * T + tile * 128:c * T + (tile + 1) * 128],
                                    wbig[sl][:, c * 512:(c + 1) * 512],
                                    start=(c == 0), stop=False)
                last = e.matmul(pp[:, :], B["ones_rowb"][0:1, :],
                                bvr[0:1, dh * 512:(dh + 1) * 512], start=False, stop=True)
                return last
            P.step("PE", _mmv, reads=[xln_b, wbigb[sl], bvr_b, B["ones_rowb_buf"]], writes=[ppb])
            s = tile % 4
            P.step("DVE", lambda e, s=s, pp=pp: e.tensor_copy(out=at[s][:, :], in_=pp[:, :]),
                   reads=[ppb], writes=[atb[s]])
            P.step("QS", lambda e, tile=tile, dh=dh, s=s: e.dma_start(
                out=mkap(B["l1a_vtd"], tile * 128 * DM + dh * 512, [[DM, 128], [1, 512]]),
                in_=at[s][:, :]),
                reads=[atb[s]], writes=[B["l1a_vtd_buf"]])

    SH = 1.0 / np.sqrt(64.0)
    for h in range(16):
        ppad = B["l1a_pp0"] if h % 2 == 0 else B["l1a_pp1"]
        ppad_b = B["l1a_pp0_buf"] if h % 2 == 0 else B["l1a_pp1_buf"]
        P.step("QS", lambda e, h=h: e.dma_start(
            out=qh[0:64, :], in_=mkap(B["l1a_qvd"], h * 64 * T, [[T, 64], [1, T]])),
            reads=[B["l1a_qvd_buf"]], writes=[qh_b])
        P.step("QS", lambda e, h=h: e.dma_start(
            out=ph[0:64, :], in_=mkap(B["l1a_posd"], h * 64 * T, [[T, 64], [1, T]])),
            reads=[B["l1a_posd_buf"]], writes=[ph_b])
        for st in range(NT):
            for jb in range(nblk):
                pp, ppb = pb[4 + (jb % 2)], pbuf[4 + (jb % 2)]
                P.step("PE", lambda e, st=st, jb=jb, pp=pp: e.matmul(
                    pp[:, :], qh[0:64, st * 128:(st + 1) * 128],
                    ph[0:64, jb * 512:(jb + 1) * 512], start=True, stop=True),
                    reads=[qh_b, ph_b], writes=[ppb])
                s = (st * nblk + jb) % 4
                P.step("DVE", lambda e, s=s, pp=pp: e.tensor_copy(out=at[s][:, :], in_=pp[:, :]),
                       reads=[ppb], writes=[atb[s]])
                P.step("QS", lambda e, st=st, jb=jb, s=s, ppad=ppad: e.dma_start(
                    out=mkap(ppad, st * 128 * 2049 + 1 + jb * 512, [[2049, 128], [1, 512]]),
                    in_=at[s][:, :]),
                    reads=[atb[s]], writes=[ppad_b])
        P.step("QS", lambda e, h=h: e.dma_start(
            out=qh[0:64, :], in_=mkap(B["l1a_qud"], h * 64 * T, [[T, 64], [1, T]])),
            reads=[B["l1a_qud_buf"]], writes=[qh_b])
        P.step("QS", lambda e, h=h: e.dma_start(
            out=kh[0:64, :], in_=mkap(B["l1a_kd"], h * 64 * T, [[T, 64], [1, T]])),
            reads=[B["l1a_kd_buf"]], writes=[kh_b])
        for skt in range(NT):
            r = skt % 2
            P.step("QS", lambda e, skt=skt, h=h, r=r: e.dma_start(
                out=vt[:, r * 65:r * 65 + 64],
                in_=mkap(B["l1a_vtd"], skt * 128 * DM + h * 64, [[DM, 128], [1, 64]])),
                reads=[B["l1a_vtd_buf"]], writes=[vt_b])
            for sqb in range(nblk):
                pp, ppb = pb[4 + (sqb % 2)], pbuf[4 + (sqb % 2)]
                pav, pavb = pb[sqb], pbuf[sqb]
                P.step("PE", lambda e, skt=skt, sqb=sqb, pp=pp: e.matmul(
                    pp[:, :], kh[0:64, skt * 128:(skt + 1) * 128],
                    qh[0:64, sqb * 512:(sqb + 1) * 512], start=True, stop=False),
                    reads=[kh_b, qh_b], writes=[ppb])
                P.step("QG", lambda e, skt=skt, sqb=sqb, ppad=ppad: e.dma_start(
                    out=sh[:, :],
                    in_=mkap(ppad, (sqb * 512) * 2048 + 2047 + skt * 128,
                             [[2048, 128], [128 * 2048, 4], [1, 128]])),
                    reads=[ppad_b], writes=[sh_b])
                def _tr(e, pp=pp):
                    last = None
                    for sub in range(4):
                        last = e.matmul(pp[:, sub * 128:(sub + 1) * 128],
                                        sh[:, sub * 128:(sub + 1) * 128], idn,
                                        start=False, stop=(sub == 3), is_transpose=True)
                    return last
                P.step("PE", _tr, reads=[sh_b, idn_b], writes=[ppb])
                P.step("ACT", lambda e, pp=pp: e.activation(
                    out=pt[:, 0:512], in_=pp[:, :], func=AF.Exp, scale=cc[:, 1:2]),
                    reads=[ppb, cc_b], writes=[pt_b])
                P.step("PE", lambda e, h=h, skt=skt, pav=pav, r=r: e.matmul(
                    pav[0:65, :], vt[:, r * 65:(r + 1) * 65], pt[:, 0:512],
                    start=(skt == 0), stop=(skt == NT - 1)),
                    reads=[vt_b, pt_b], writes=[pavb])
        for sqb in range(nblk):
            pav, pavb = pb[sqb], pbuf[sqb]
            P.step("DVE", lambda e, pav=pav: e.tensor_copy(out=av[0:65, 0:512], in_=pav[0:65, :]),
                   reads=[pavb], writes=[av_b])
            P.step("DVE", lambda e: e.reciprocal(out=av[64:65, 0:512], in_=av[64:65, 0:512]),
                   reads=[av_b], writes=[av_b])
            pbc, pbcb = pb[6], pbuf[6]
            P.step("PE", lambda e: e.matmul(
                pbc[0:128, 0:512], B["ones_sqf"][64:65, 0:128], av[64:65, 0:512],
                start=True, stop=True),
                reads=[av_b, B["ones_sqf_buf"]], writes=[pbcb])
            P.step("DVE", lambda e: e.tensor_tensor(
                out=av[0:64, 0:512], in0=av[0:64, 0:512], in1=pbc[0:64, :], op=AO.mult),
                reads=[pbcb, av_b], writes=[av_b])
            P.step("QG", lambda e, h=h, sqb=sqb: e.dma_start(
                out=mkap(B["l1a_aod"], h * 64 * T + sqb * 512, [[T, 64], [1, 512]]),
                in_=av[0:64, 0:512]),
                reads=[av_b], writes=[B["l1a_aod_buf"]])

    def wo_load(oc):
        sl = oc % 4
        src_ap = mkap(B["l1a_wo"], oc * DM * 128, [[128, 128], [128 * 128, NC_DM], [1, 128]])
        P.step("QG", lambda e, sl=sl, src_ap=src_ap: e.dma_start(
            out=ws[sl].rearrange("p (c m) -> p c m", c=NC_DM), in_=src_ap),
            reads=[], writes=[wsb[sl]])
    wo_load(0); wo_load(1)
    for oc in range(NC_DM):
        sl = oc % 4
        if oc + 2 < NC_DM:
            wo_load(oc + 2)
        for tb in range(nblk):
            r = tb % 2
            pp, ppb = pb[4 + r], pbuf[4 + r]
            def _mm(e, sl=sl, tb=tb, pp=pp):
                last = None
                for c in range(NC_DM):
                    last = e.matmul(pp[:, :], ws[sl][:, c * 128:(c + 1) * 128],
                                    abuf[:, c * 512:(c + 1) * 512],
                                    start=(c == 0), stop=(c == NC_DM - 1))
                return last
            for c in range(NC_DM):
                P.step("QS", lambda e, c=c, tb=tb: e.dma_start(
                    out=abuf[:, c * 512:(c + 1) * 512],
                    in_=mkap(B["l1a_aod"], c * 128 * T + tb * 512, [[T, 128], [1, 512]])),
                    reads=[B["l1a_aod_buf"]], writes=[abuf_b])
            P.step("PE", _mm, reads=[wsb[sl], abuf_b], writes=[ppb])
            xs = xres[:, oc * T + tb * 512: oc * T + (tb + 1) * 512]
            P.step("DVE", lambda e, oc=oc, xs=xs, pp=pp: e.scalar_tensor_tensor(
                out=xs, in0=pp[:, :], scalar=cols[:, 32 + oc:32 + oc + 1],
                in1=xs, op0=AO.add, op1=AO.add),
                reads=[ppb, cols_b, xres_b], writes=[xres_b])





def prep_l0attn(inp):
    d = {}
    d["l0a_wq"] = tile_lhs(np.ascontiguousarray(np.asarray(inp["L0_attn_Wq"]).T))
    d["l0a_wk"] = tile_lhs(np.ascontiguousarray(np.asarray(inp["L0_attn_Wk"]).T))
    d["l0a_wv"] = tile_lhs(np.ascontiguousarray(np.asarray(inp["L0_attn_Wv"]).T))
    for nm, key in [("l0a_kpk", "L0_attn_k_kernel"), ("l0a_kps", "L0_attn_ssm_kernel")]:
        ker = np.asarray(inp[key])
        kp = np.zeros((1024, 2176), np.float32)
        kp[:, 0:2048] = ker[:, ::-1]
        d[nm] = kp
    d["l0a_bq"] = col_tiles(inp["L0_attn_bq"])
    d["l0a_bk"] = col_tiles(inp["L0_attn_bk"])
    d["l0a_bv"] = col_tiles(inp["L0_attn_bv"])
    d["l0a_kD"] = np.asarray(inp["L0_attn_k_D"]).astype(np.float32)
    d["l0a_D"] = np.asarray(inp["L0_attn_D"]).astype(np.float32)
    return d


def core_slice_l0attn(d, core):
    out = {}
    out["l0a_wq_c"] = np.ascontiguousarray(d["l0a_wq"][core])
    out["l0a_wk_c"] = np.ascontiguousarray(d["l0a_wk"][core])
    out["l0a_wv_c"] = np.ascontiguousarray(d["l0a_wv"][core])
    out["l0a_kpk_c"] = np.ascontiguousarray(d["l0a_kpk"][core * 128:(core + 1) * 128])
    out["l0a_kps_c"] = np.ascontiguousarray(d["l0a_kps"][core * 128:(core + 1) * 128])
    cols = np.zeros((128, 8), np.float32)
    cols[:, 0] = d["l0a_bq"][:, core]
    cols[:, 1] = d["l0a_bk"][:, core]
    cols[:, 2] = d["l0a_bv"][:, core]
    out["l0a_cols"] = cols
    rows = np.zeros((1, 256), np.float32)
    rows[0, 0:128] = d["l0a_kD"][core * 128:(core + 1) * 128]
    rows[0, 128:256] = d["l0a_D"][core * 128:(core + 1) * 128]
    out["l0a_rows"] = rows
    out["l0a_rev"] = np.eye(128, dtype=np.float32)[::-1].copy()
    return out


def declare_l0mix(nc, B, ctxs):
    names = []
    for nm, shape in [("l0a_wq_c", [DM, 128]), ("l0a_wk_c", [DM, 128]),
                      ("l0a_wv_c", [DM, 128]), ("l0a_kpk_c", [128, 2176]),
                      ("l0a_kps_c", [128, 2176]), ("l0a_cols", [128, 8]),
                      ("l0a_rows", [1, 256]), ("l0a_rev", [128, 128])]:
        B[nm] = nc.dram_tensor(nm, shape, F32, kind="ExternalInput").ap()
        names.append(nm)
    B["l0a_z"] = nc.dram_tensor("l0a_z", [DM, 16384], BF16, kind="ExternalInput").ap()
    names.append("l0a_z")
    for nm, shape, sdt in [("l0wp", [128, 3072], BF16), ("l0rows_s", [1, 256], F32),
                           ("l0cols_s", [128, 8], F32), ("l0revb", [128, 128], BF16),
                           ("l0hanK0", [128, 2048], BF16), ("l0hanK1", [128, 2048], BF16),
                           ("l0hanS0", [128, 2048], BF16), ("l0hanS1", [128, 2048], BF16),
                           ("l0kt", [128, 256], BF16), ("l0vt", [128, 256], BF16),
                           ("l0qt", [128, 256], BF16),
                           ("l0kvh", [128, 128], BF16), ("l0kvn", [128, 128], BF16),
                           ("l0sc1", [128, 128], F32), ("l0sc2", [128, 128], BF16),
                           ("l0sc3", [128, 128], F32), ("l0cc", [128, 4], F32),
                           ("l0ev", [128, 1024], BF16), ("l0yt", [128, 256], BF16),
                           ("l0revf", [128, 128], F32),
                           ("l0zt", [128, 4096], BF16)]:
        cm = nc.sbuf_tensor(nm, shape, sdt)
        h = cm.__enter__()
        ctxs.append(cm)
        B[nm] = h[:]
        B[nm + "_buf"] = Buf(nm)
    for nm in ["l0a_kd", "l0a_vd", "l0a_qd", "l0a_yd"]:
        B[nm] = nc.dram_tensor(nm, [128, 128, 128], BF16).ap()
        B[nm + "_buf"] = Buf(nm)
    return names


def emit_l0mix(P, B):
    pb, pbuf = B["psum"], B["psum_buf"]
    wp, wp_b = B["l0wp"], B["l0wp_buf"]
    rows, rows_b = B["l0rows_s"], B["l0rows_s_buf"]
    lcols, lcols_b = B["l0cols_s"], B["l0cols_s_buf"]
    revb, revb_b = B["l0revb"], B["l0revb_buf"]
    hanK = [B["l0hanK0"], B["l0hanK1"]]
    hanKb = [B["l0hanK0_buf"], B["l0hanK1_buf"]]
    hanS = [B["l0hanS0"], B["l0hanS1"]]
    hanSb = [B["l0hanS0_buf"], B["l0hanS1_buf"]]
    kt, kt_b = B["l0kt"], B["l0kt_buf"]
    vt, vt_b = B["l0vt"], B["l0vt_buf"]
    qt, qt_b = B["l0qt"], B["l0qt_buf"]
    kvh, kvh_b = B["l0kvh"], B["l0kvh_buf"]
    kvn, kvn_b = B["l0kvn"], B["l0kvn_buf"]
    sc1, sc1_b = B["l0sc1"], B["l0sc1_buf"]
    sc2, sc2_b = B["l0sc2"], B["l0sc2_buf"]
    sc3, sc3_b = B["l0sc3"], B["l0sc3_buf"]
    cc2, cc2_b = B["l0cc"], B["l0cc_buf"]
    ev, ev_b = B["l0ev"], B["l0ev_buf"]
    yt, yt_b = B["l0yt"], B["l0yt_buf"]
    zt, zt_b = B["l0zt"], B["l0zt_buf"]
    idn, idn_b = B["idn128"], B["idn128_buf"]
    ones_row, ones_row_b = B["ones_row"], B["ones_row_buf"]

    P.step("QS", lambda e: e.dma_start(out=rows, in_=B["l0a_rows"]), reads=[], writes=[rows_b])
    P.step("QS", lambda e: e.dma_start(out=lcols, in_=B["l0a_cols"]), reads=[], writes=[lcols_b])
    P.step("QG", lambda e: e.dma_start(out=revb, in_=B["l0a_rev"]), reads=[], writes=[revb_b])
    P.step("QS", lambda e: e.dma_start(out=B["l0revf"], in_=B["l0a_rev"]),
           reads=[], writes=[B["l0revf_buf"]])
    for pi, wkey in enumerate(["l0a_wq_c", "l0a_wk_c", "l0a_wv_c"]):
        P.step("QG", lambda e, pi=pi, wkey=wkey: e.dma_start(
            out=wp[:, pi * 1024:(pi + 1) * 1024].rearrange("p (c m) -> p c m", c=NC_DM),
            in_=mkap(B[wkey], 0, [[128, 128], [128 * 128, NC_DM], [1, 128]])),
            reads=[], writes=[wp_b])

    for blk in range(32):
        P.step("QS", lambda e, blk=blk: e.dma_start(
            out=zt.rearrange("p (c t) -> p c t", c=NC_DM),
            in_=mkap(B["l0a_z"], blk * 512, [[16384, 128], [128 * 16384, NC_DM], [1, 512]])),
            reads=[], writes=[zt_b])
        b_row, j0 = blk // 4, (blk % 4) * 4
        for pi, dkey in enumerate(["l0a_qd", "l0a_kd", "l0a_vd"]):
            pp, ppb = pb[4 + pi % 2], pbuf[4 + pi % 2]
            def _mm(e, pi=pi, pp=pp):
                last = None
                for c in range(NC_DM):
                    last = e.matmul(pp[:, :], wp[:, pi * 1024 + c * 128:pi * 1024 + (c + 1) * 128],
                                    zt[:, c * 512:(c + 1) * 512],
                                    start=(c == 0), stop=(c == NC_DM - 1))
                return last
            P.step("PE", _mm, reads=[wp_b, zt_b], writes=[ppb])
            eslot = ev[:, (pi % 2) * 512:(pi % 2) * 512 + 512]
            P.step("DVE", lambda e, pi=pi, pp=pp, eslot=eslot: e.tensor_scalar(
                out=eslot, in0=pp[:, :], scalar1=lcols[:, pi:pi + 1], scalar2=None, op0=AO.add),
                reads=[ppb, lcols_b], writes=[ev_b])
            P.step("QS", lambda e, dkey=dkey, b_row=b_row, j0=j0, pi=pi: e.dma_start(
                out=mkap(B[dkey], (j0 * 8 + b_row) * 128, [[16384, 128], [1024, 4], [1, 128]]),
                in_=mkap(ev, (pi % 2) * 512, [[1024, 128], [128, 4], [1, 128]])),
                reads=[ev_b], writes=[B[dkey + "_buf"]])

    def han_load(ch, which, sl):
        key = "l0a_kpk_c" if which == 0 else "l0a_kps_c"
        buf = hanK[sl] if which == 0 else hanS[sl]
        bb = hanKb[sl] if which == 0 else hanSb[sl]
        P.step("QG", lambda e, ch=ch, key=key, buf=buf: e.dma_start(
            out=buf.rearrange("p (g m) -> p g m", g=16),
            in_=mkap(B[key], ch * 2176, [[1, 128], [128, 16], [1, 128]])),
            reads=[], writes=[bb])

    han_load(0, 0, 0); han_load(0, 1, 0)
    for ch in range(128):
        sl = ch % 2
        r0 = (ch % 2) * 3
        if ch + 1 < 128:
            han_load(ch + 1, 0, 1 - sl); han_load(ch + 1, 1, 1 - sl)
        for tkey, tile, tb_ in [("l0a_kd", kt, kt_b), ("l0a_vd", vt, vt_b), ("l0a_qd", qt, qt_b)]:
            P.step("QS", lambda e, ch=ch, tkey=tkey, tile=tile, sl=sl: e.dma_start(
                out=tile[:, sl * 128:(sl + 1) * 128],
                in_=mkap(B[tkey], ch * 16384, [[128, 128], [1, 128]]), transpose=True),
                reads=[B[tkey + "_buf"]], writes=[tb_])
        kts = kt[:, sl * 128:(sl + 1) * 128]
        vts = vt[:, sl * 128:(sl + 1) * 128]
        qts = qt[:, sl * 128:(sl + 1) * 128]
        p1, p1b = pb[r0], pbuf[r0]
        def _cv1(e, sl=sl, p1=p1):
            last = None
            for g in range(16):
                n = 128 - 8 * g
                last = e.matmul(p1[:, 8 * g:128], hanK[sl][:, (15 - g) * 128:(16 - g) * 128],
                                kts[:, 0:n], start=(g == 0), stop=(g == 15))
            return last
        P.step("PE", _cv1, reads=[hanKb[sl], kt_b], writes=[p1b])
        pbc, pbcb = pb[6], pbuf[6]
        P.step("PE", lambda e, ch=ch, pbc=pbc: e.matmul(
            pbc[:, 0:1], ones_row[0:1, :], rows[0:1, ch:ch + 1], start=True, stop=True),
            reads=[rows_b, ones_row_b], writes=[pbcb])
        P.step("PE", lambda e, ch=ch, pbc=pbc: e.matmul(
            pbc[:, 1:2], ones_row[0:1, :], rows[0:1, 128 + ch:128 + ch + 1], start=True, stop=True),
            reads=[rows_b, ones_row_b], writes=[pbcb])
        P.step("DVE", lambda e, pbc=pbc: e.tensor_copy(out=cc2[:, 0:2], in_=pbc[:, 0:2]),
               reads=[pbcb], writes=[cc2_b])
        p2, p2b = pb[r0 + 1], pbuf[r0 + 1]
        P.step("PE", lambda e, p2=p2, kts=kts: e.matmul(
            p2[:, 0:128], revb, kts, start=True, stop=True),
            reads=[revb_b, kt_b], writes=[p2b])
        P.step("DVE", lambda e, p2=p2: e.tensor_scalar(
            out=sc2[:, :], in0=p2[:, 0:128], scalar1=cc2[:, 0:1], scalar2=None, op0=AO.mult),
            reads=[p2b, cc2_b], writes=[sc2_b])
        P.step("PE", lambda e, p2=p2, vts=vts: e.matmul(
            p2[:, 0:128], revb, vts, start=True, stop=True),
            reads=[revb_b, vt_b, sc2_b], writes=[p2b])
        P.step("DVE", lambda e, p1=p1: e.tensor_tensor(
            out=sc1[:, :], in0=p1[:, 0:128], in1=sc2[:, :], op=AO.add),
            reads=[p1b, sc2_b], writes=[sc1_b])
        P.step("DVE", lambda e, p2=p2: e.tensor_tensor(
            out=kvh[:, :], in0=sc1[:, :], in1=p2[:, 0:128], op=AO.mult),
            reads=[sc1_b, p2b], writes=[kvh_b])
        p3, p3b = pb[r0 + 2], pbuf[r0 + 2]
        P.step("PE", lambda e, p3=p3: e.matmul(
            p3[:, 0:128], revb, kvh[:, :], start=True, stop=True),
            reads=[revb_b, kvh_b], writes=[p3b])
        P.step("DVE", lambda e, p3=p3: e.tensor_copy(out=kvn[:, :], in_=p3[:, 0:128]),
               reads=[p3b], writes=[kvn_b])
        def _cv2(e, sl=sl, p1=p1):
            last = None
            for g in range(16):
                n = 128 - 8 * g
                last = e.matmul(p1[:, 8 * g:128], hanS[sl][:, (15 - g) * 128:(16 - g) * 128],
                                kvn[:, 0:n], start=(g == 0), stop=(g == 15))
            return last
        P.step("PE", _cv2, reads=[hanSb[sl], kvn_b, sc1_b], writes=[p1b])
        P.step("DVE", lambda e: e.tensor_scalar(
            out=sc2[:, :], in0=kvh[:, :], scalar1=cc2[:, 1:2], scalar2=None, op0=AO.mult),
            reads=[kvh_b, cc2_b], writes=[sc2_b])
        P.step("PE", lambda e, p2=p2, qts=qts: e.matmul(
            p2[:, 0:128], revb, qts, start=True, stop=True),
            reads=[revb_b, qt_b], writes=[p2b])
        P.step("DVE", lambda e, p1=p1: e.tensor_tensor(
            out=sc1[:, :], in0=p1[:, 0:128], in1=sc2[:, :], op=AO.add),
            reads=[p1b, sc2_b], writes=[sc1_b])
        P.step("DVE", lambda e, p2=p2: e.tensor_tensor(
            out=sc3[:, :], in0=sc1[:, :], in1=p2[:, 0:128], op=AO.mult),
            reads=[sc1_b, p2b], writes=[sc3_b])
        p4, p4b = pb[r0 + 1], pbuf[r0 + 1]
        P.step("PE", lambda e, p4=p4: e.matmul(
            p4[:, 0:128], sc3[:, :], B["l0revf"], start=True, stop=True, is_transpose=True),
            reads=[sc3_b, B["l0revf_buf"]], writes=[p4b])
        yts = yt[:, sl * 128:(sl + 1) * 128]
        P.step("DVE", lambda e, p4=p4, yts=yts: e.tensor_copy(out=yts, in_=p4[:, 0:128]),
               reads=[p4b], writes=[yt_b])
        P.step("QS", lambda e, ch=ch, yts=yts: e.dma_start(
            out=mkap(B["l0a_yd"], ch * 16384, [[128, 128], [1, 128]]),
            in_=yts),
            reads=[yt_b], writes=[B["l0a_yd_buf"]])





def declare_l0wo(nc, B, T, ctxs):
    names = []
    B["l0wo_w"] = nc.dram_tensor("l0wo_w", [8, DM, 128], F32, kind="ExternalInput").ap()
    B["l0wo_bo"] = nc.dram_tensor("l0wo_bo", [128, 8], F32, kind="ExternalInput").ap()
    B["l0y_in"] = nc.dram_tensor("l0y_in", [DM, T], BF16, kind="ExternalInput").ap()
    names += ["l0wo_w", "l0wo_bo", "l0y_in"]
    cm = nc.sbuf_tensor("l0wo_bo_s", [128, 8], F32)
    h = cm.__enter__()
    ctxs.append(cm)
    B["l0wo_bo_s"] = h[:]
    B["l0wo_bo_s_buf"] = Buf("l0wo_bo_s")
    return names


def prep_l0wo(inp):
    return {"l0wo_w": tile_lhs(np.ascontiguousarray(np.asarray(inp["L0_attn_Wo"]).T)),
            "l0wo_bo": col_tiles(inp["L0_attn_bo"])}


def emit_l0wo(P, B, T):
    xres, xres_b = B["xres"], B["xres_buf"]
    pb, pbuf = B["psum"], B["psum_buf"]
    ws = [B[f"ws{i}"] for i in range(4)]
    wsb = [B[f"ws{i}_buf"] for i in range(4)]
    abuf, abuf_b = B["abuf"], B["abuf_buf"]
    bo, bo_b = B["l0wo_bo_s"], B["l0wo_bo_s_buf"]
    nblk = T // 512
    P.step("QS", lambda e: e.dma_start(out=bo, in_=B["l0wo_bo"]), reads=[], writes=[bo_b])
    def wload(oc):
        sl = oc % 4
        src_ap = mkap(B["l0wo_w"], oc * DM * 128, [[128, 128], [128 * 128, NC_DM], [1, 128]])
        P.step("QG", lambda e, sl=sl, src_ap=src_ap: e.dma_start(
            out=ws[sl].rearrange("p (c m) -> p c m", c=NC_DM), in_=src_ap),
            reads=[], writes=[wsb[sl]])
    wload(0); wload(1)
    for oc in range(NC_DM):
        sl = oc % 4
        if oc + 2 < NC_DM:
            wload(oc + 2)
        for tb in range(nblk):
            for c in range(NC_DM):
                P.step("QS", lambda e, c=c, tb=tb: e.dma_start(
                    out=abuf[:, c * 512:(c + 1) * 512],
                    in_=mkap(B["l0y_in"], c * 128 * T + tb * 512, [[T, 128], [1, 512]])),
                    reads=[], writes=[abuf_b])
            r = tb % 2
            pp, ppb = pb[4 + r], pbuf[4 + r]
            def _mm(e, sl=sl, pp=pp):
                last = None
                for c in range(NC_DM):
                    last = e.matmul(pp[:, :], ws[sl][:, c * 128:(c + 1) * 128],
                                    abuf[:, c * 512:(c + 1) * 512],
                                    start=(c == 0), stop=(c == NC_DM - 1))
                return last
            P.step("PE", _mm, reads=[wsb[sl], abuf_b], writes=[ppb])
            xs = xres[:, oc * T + tb * 512: oc * T + (tb + 1) * 512]
            P.step("DVE", lambda e, oc=oc, xs=xs, pp=pp: e.scalar_tensor_tensor(
                out=xs, in0=pp[:, :], scalar=bo[:, oc:oc + 1],
                in1=xs, op0=AO.add, op1=AO.add),
                reads=[ppb, bo_b, xres_b], writes=[xres_b])



def emit_add_fm(P, B, T, dram_key):
    xres, xres_b = B["xres"], B["xres_buf"]
    tf, tf_b = B["t512f"], B["t512f_buf"]
    for c in range(NC_DM):
        for tb in range(T // 512):
            P.step("QS", lambda e, c=c, tb=tb: e.dma_start(
                out=tf[:, :],
                in_=mkap(B[dram_key], c * 128 * T + tb * 512, [[T, 128], [1, 512]])),
                reads=[], writes=[tf_b])
            xs = xres[:, c * T + tb * 512: c * T + (tb + 1) * 512]
            P.step("DVE", lambda e, xs=xs: e.tensor_tensor(
                out=xs, in0=xs, in1=tf[:, :], op=AO.add),
                reads=[tf_b, xres_b], writes=[xres_b])


def build_dev_l0mix():
    nc = bass.Bass("TRN2", target_bir_lowering=False)
    ctxs = []
    B = {}
    def ps(name, shape, dt=F32):
        cm = nc.psum_tensor(name, shape, dt)
        h = cm.__enter__()
        ctxs.append(cm)
        return h
    for nm, shape, sdt in [("ones_row", [1, 128], F32), ("idn128", [128, 128], F32)]:
        cm = nc.sbuf_tensor(nm, shape, sdt)
        h = cm.__enter__()
        ctxs.append(cm)
        B[nm] = h[:]
        B[nm + "_buf"] = Buf(nm)
    B["idn128d"] = nc.dram_tensor("idn128d", [128, 128], F32, kind="ExternalInput").ap()
    declare_l0mix(nc, B, ctxs)
    B["l0a_yd"] = nc.dram_tensor("l0a_yd_ext", [128, 128, 128], BF16, kind="ExternalOutput").ap()
    B["l0a_yd_buf"] = Buf("l0a_yd_ext")
    B["psum"] = [ps(f"pb{i}", [128, 512]) for i in range(8)]
    B["psum_buf"] = [Buf(f"pb{i}") for i in range(8)]
    sems = {}
    for name in ALL_SEMS:
        scm = nc.semaphore("sem_" + name)
        sems[name] = scm.__enter__()
        ctxs.append(scm)
    bcm = nc.Block()
    block = bcm.__enter__()
    ctxs.append(bcm)
    P = Prog(nc, block, sems)
    P.step("DVE", lambda e: e.memset(B["ones_row"][0:1, :], 1.0), writes=[B["ones_row_buf"]])
    P.step("QS", lambda e: e.dma_start(out=B["idn128"], in_=B["idn128d"]),
           reads=[], writes=[B["idn128_buf"]])
    emit_l0mix(P, B)
    P.drain("QS")
    P.drain("POOL")
    for c in ctxs[::-1]:
        c.__exit__(None, None, None)
    return nc


def build_row_launch(T, ops):
    nc = bass.Bass("TRN2", target_bir_lowering=False)
    ctxs = []
    names = ["xin", "idn128d"]
    xin = nc.dram_tensor("xin", [DM, T], F32, kind="ExternalInput").ap()
    xout = nc.dram_tensor("xout", [DM, T], F32, kind="ExternalOutput").ap()
    kinds = [o.split(":")[0] for o in ops]
    moe = any(k in kinds for k in ("l1ffn", "l3ffn"))
    l1 = "l1ffn" in kinds
    B = alloc_shared(nc, ctxs, T, moe=moe, l1=l1)
    declare_common(nc, B)
    B["scr0"] = nc.dram_tensor("scr0", [4096, T], BF16).ap()
    B["scr0_buf"] = Buf("scr0")
    B["scr1"] = nc.dram_tensor("scr1", [4096, T], BF16).ap()
    B["scr1_buf"] = Buf("scr1")
    for op in ops:
        k = op.split(":")[0]
        if k == "l0ffn" and "l0f_w1" not in B:
            cm = nc.sbuf_tensor("l0f_sc", [128, 137], F32)
            B["l0f_sc"] = cm.__enter__()[:]
            ctxs.append(cm)
            B["l0f_sc_buf"] = Buf("l0f_sc")
            for nm, shape in [("l0f_w1", [32, DM, 128]), ("l0f_w2", [32, DM, 128]),
                              ("l0f_h1", [32, 4096, 128]), ("l0f_h2", [32, 4096, 128]),
                              ("l0f_w3", [8, 4096, 128])]:
                B[nm] = nc.dram_tensor(nm, shape, F32, kind="ExternalInput").ap()
                names.append(nm)
            B["l0f_sc_d"] = nc.dram_tensor("l0f_sc_d", [128, 137], F32, kind="ExternalInput").ap()
            names.append("l0f_sc_d")
        elif k == "l1ffn" and "l1f_w1" not in B:
            names += declare_l1ffn(nc, B, T, ctxs)
        elif k == "l2ffn" and "l2f_wg" not in B:
            names += declare_l2ffn(nc, B, T)
        elif k == "l3ffn" and "l3f_w1" not in B:
            names += declare_l3ffn(nc, B, T, ctxs)
        elif k == "l1attn":
            names += declare_l1attn(nc, B, T, ctxs)
        elif k == "l2attn":
            names += declare_l2attn(nc, B, T, ctxs)
            B["l2a_out"] = nc.dram_tensor("l2a_out_ext", [2048, DM], F32, kind="ExternalOutput").ap()
        elif k == "l3attn":
            names += declare_l3attn(nc, B, T, ctxs)
            B["l3a_y"] = nc.dram_tensor("l3a_y_ext", [DM, T], F32, kind="ExternalOutput").ap()
        elif k == "l0wo":
            names += declare_l0wo(nc, B, T, ctxs)
        elif k == "addfm":
            nm = op.split(":")[1]
            B[nm] = nc.dram_tensor(nm, [DM, T], F32, kind="ExternalInput").ap()
            names.append(nm)
        elif k == "zout":
            B["zout"] = nc.dram_tensor("zout", [DM, T], BF16, kind="ExternalOutput").ap()
    sems = {}
    for name in ALL_SEMS:
        scm = nc.semaphore("sem_" + name)
        sems[name] = scm.__enter__()
        ctxs.append(scm)
    bcm = nc.Block()
    block = bcm.__enter__()
    ctxs.append(bcm)
    P = Prog(nc, block, sems)
    emit_const_init(P, B)
    if "l0ffn" in kinds:
        emit_l0ffn_load_consts(P, B)
    emit_load_fm(P, B, T, xin, B["xres"], B["xres_buf"])
    for op in ops:
        k = op.split(":")[0]
        if k in ("l0ffn", "l1ffn", "l2ffn", "l3ffn", "l1attn", "l2attn", "l3attn"):
            emit_ln(P, B, T, B["xres"], B["xres_buf"], B["xln"], B["xln_buf"])
        if k == "l0ffn":
            emit_l0ffn(P, B, T)
        elif k == "l1ffn":
            emit_l1ffn(P, B, T)
        elif k == "l2ffn":
            emit_l2ffn(P, B, T)
        elif k == "l3ffn":
            emit_l3ffn(P, B, T)
        elif k == "l1attn":
            emit_l1attn(P, B, T)
        elif k == "l2attn":
            emit_l2attn(P, B, T)
        elif k == "l3attn":
            emit_l3attn(P, B, T)
        elif k == "l0wo":
            emit_l0wo(P, B, T)
        elif k == "addfm":
            emit_add_fm(P, B, T, op.split(":")[1])
        elif k == "zout":
            emit_ln(P, B, T, B["xres"], B["xres_buf"], B["xln"], B["xln_buf"])
            emit_store_fm(P, B, T, B["xln"], B["xln_buf"], B["zout"])
    emit_store_fm(P, B, T, B["xres"], B["xres_buf"], xout)
    P.drain("QS")
    P.drain("POOL")
    for c in ctxs[::-1]:
        c.__exit__(None, None, None)
    return nc, names


_LAUNCH_CACHE = {}


def kernel(**inputs):
    inp = {k: np.asarray(v) for k, v in inputs.items()}
    T = 2048
    x0 = inp["L0_attn_u"].astype(np.float32)
    idn = np.eye(128, dtype=np.float32)
    WS = {}
    WS.update(prep_l0ffn(inp))
    WS["l0f_sc_d"] = WS.pop("l0f_sc")
    WS.update(prep_l0wo(inp))
    WS.update(prep_l1ffn(inp))
    WS.update(prep_l1attn(inp, T))
    WS.update(prep_l2ffn(inp))
    WS.update(prep_l2attn(inp))
    WS.update(prep_l3ffn(inp))
    WS.update(prep_l3attn(inp))
    d0 = prep_l0attn(inp)
    WS["idn128d"] = idn

    def row_launch(tag, ops, percore_extra):
        if tag not in _LAUNCH_CACHE:
            _LAUNCH_CACHE[tag] = build_row_launch(T, ops)
        nc, names = _LAUNCH_CACHE[tag]
        maps = []
        for b in range(NCORES):
            im = {}
            for nm in names:
                if nm in percore_extra[b]:
                    im[nm] = percore_extra[b][nm]
                elif nm in WS:
                    im[nm] = WS[nm]
                else:
                    raise KeyError(f"launch {tag}: missing input {nm}")
            maps.append(im)
        res = run_bass_kernel_spmd(nc, maps, core_ids=list(range(NCORES)))
        return res.results

    x = [np.ascontiguousarray(x0[b].T) for b in range(NCORES)]

    rs = row_launch("A", ["l0ffn", "zout"], [{"xin": x[b]} for b in range(NCORES)])
    x = [r["xout"] for r in rs]
    zfull = np.ascontiguousarray(np.concatenate([np.asarray(r["zout"]) for r in rs], axis=1))

    if "B" not in _LAUNCH_CACHE:
        _LAUNCH_CACHE["B"] = (build_dev_l0mix(), None)
    ncB, _ = _LAUNCH_CACHE["B"]
    mapsB = []
    for c in range(NCORES):
        im = core_slice_l0attn(d0, c)
        im["idn128d"] = idn
        im["l0a_z"] = zfull
        mapsB.append(im)
    resB = run_bass_kernel_spmd(ncB, mapsB, core_ids=list(range(NCORES))).results
    yd = [np.asarray(r["l0a_yd_ext"]) for r in resB]
    yfm = []
    for b in range(NCORES):
        arr = np.zeros((DM, T), dtype=yd[0].dtype)
        for c in range(NCORES):
            a = yd[c].reshape(128, 16, 8, 128)[:, :, b, :]
            arr[c * 128:(c + 1) * 128, :] = a.reshape(128, T)
        yfm.append(np.ascontiguousarray(arr))

    rs = row_launch("C1", ["l0wo", "l0ffn"],
                    [{"xin": x[b], "l0y_in": yfm[b]} for b in range(NCORES)])
    x = [r["xout"] for r in rs]
    rs = row_launch("C2", ["l1ffn"], [{"xin": x[b]} for b in range(NCORES)])
    x = [r["xout"] for r in rs]
    rs = row_launch("C3", ["l1attn"], [{"xin": x[b]} for b in range(NCORES)])
    x = [r["xout"] for r in rs]
    rs = row_launch("C4", ["l1ffn"], [{"xin": x[b]} for b in range(NCORES)])
    x = [r["xout"] for r in rs]
    rs = row_launch("C5", ["l2ffn", "l2attn"], [{"xin": x[b]} for b in range(NCORES)])
    x = [r["xout"] for r in rs]
    l2o = [np.asarray(r["l2a_out_ext"]) for r in rs]
    att2 = []
    cidx = 8 * np.arange(256)
    for s in range(NCORES):
        arr = np.zeros((DM, T), np.float32)
        for r in range(NCORES):
            a = l2o[r].reshape(8, 256, DM)
            arr[:, cidx + r] = a[s].T
        att2.append(np.ascontiguousarray(arr))
    rs = row_launch("D1", ["addfm:att2s", "l2ffn"],
                    [{"xin": x[b], "att2s": att2[b]} for b in range(NCORES)])
    x = [r["xout"] for r in rs]
    rs = row_launch("D2", ["l3ffn"], [{"xin": x[b]} for b in range(NCORES)])
    x = [r["xout"] for r in rs]
    rs = row_launch("D3", ["l3attn"], [{"xin": x[b]} for b in range(NCORES)])
    x = [r["xout"] for r in rs]
    yD = [np.asarray(r["l3a_y_ext"]) for r in rs]
    ys = []
    bidx = np.arange(256) * 8
    for i in range(NCORES):
        arr = np.zeros((DM, T), np.float32)
        for b in range(NCORES):
            arr[:, bidx + b] = yD[b][:, 256 * i:256 * (i + 1)]
        ys.append(np.ascontiguousarray(arr))
    rs = row_launch("E", ["addfm:ys", "l3ffn"],
                    [{"xin": x[b], "ys": ys[b]} for b in range(NCORES)])
    out = np.stack([np.asarray(r["xout"]).T for r in rs]).astype(np.float32)
    return out
